# version 10
# baseline (speedup 1.0000x reference)
import numpy as np
import concourse.bass as bass
import concourse.mybir as mybir
from concourse.bass_utils import run_bass_kernel_spmd

F32 = mybir.dt.float32
BF16 = mybir.dt.bfloat16
AF = mybir.ActivationFunctionType
ALU = mybir.AluOpType
AX = mybir.AxisListType

B, SEQ, D = 2, 8192, 2048
H, HD = 16, 128
SCALE = 1.0 / float(np.sqrt(HD))
NE = 32
FH = 768
NI = 768
TLOC = 2048
TP = 1024
NDC = D // 128
NFC = FH // 128
NIC = NI // 128
EPS = 1e-5
BIG = 1e30

ENGINES = ("sync", "tensor", "vector", "scalar", "gpsimd")
NDMASEM = 6


class Sched:

    def __init__(self, nc, es):
        self.nc = nc
        self.esem = {}
        self.key2sem = {}
        for e in ENGINES:
            s = es.enter_context(nc.semaphore(f"prog_{e}"))
            self.esem[e] = ("E" + e, s)
            self.key2sem["E" + e] = s
        self.dsem = {}
        for e in ENGINES:
            lst = []
            for i in range(NDMASEM):
                s = es.enter_context(nc.semaphore(f"dma_{e}_{i}"))
                key = f"D{e}{i}"
                lst.append((key, s))
                self.key2sem[key] = s
            self.dsem[e] = lst
        self.cnt = {}
        self.dmause = {e: 0 for e in ENGINES}
        self.streams = {e: [] for e in ENGINES}
        self.clock = {e: {} for e in ENGINES}
        self.snap = {}
        self.regions = {}

    def add_sem(self, name, es):
        s = es.enter_context(self.nc.semaphore(name))
        key = "X" + name
        self.key2sem[key] = s
        return (key, s)

    def _edge(self, need, token):
        k, v = token
        if need.get(k, 0) < v:
            need[k] = v

    def op(self, eng, fn, reads=(), writes=(), dma=False, extra_tokens=(),
           sem_override=None):
        need = {}
        for r in reads:
            reg = self.regions.get(r)
            if reg:
                for t in reg[0]:
                    self._edge(need, t)
        for w in writes:
            reg = self.regions.get(w)
            if reg:
                for t in reg[0]:
                    self._edge(need, t)
                for rk, rv in reg[1].items():
                    self._edge(need, (rk, rv))
        for t in extra_tokens:
            self._edge(need, t)
        if sem_override is not None:
            key, sem = sem_override
            delta = 1
            myval = self.cnt.get(key, 0) + 1
        elif dma:
            i = self.dmause[eng] % NDMASEM
            self.dmause[eng] += 1
            key, sem = self.dsem[eng][i]
            prev = self.cnt.get(key, 0)
            if prev:
                self._edge(need, (key, prev))
            delta = 16
            myval = prev + 16
        else:
            key, sem = self.esem[eng]
            delta = 1
            myval = self.cnt.get(key, 0) + 1
        clk = self.clock[eng]
        waits = []
        for k, v in sorted(need.items()):
            if clk.get(k, 0) < v:
                waits.append((self.key2sem[k], v))
                psnap = self.snap.get((k, v))
                if psnap:
                    for kk, vv in psnap.items():
                        if clk.get(kk, 0) < vv:
                            clk[kk] = vv
                if clk.get(k, 0) < v:
                    clk[k] = v
        self.cnt[key] = myval
        self.streams[eng].append((fn, waits, key, sem, delta, myval))
        snapshot = dict(clk)
        snapshot[key] = myval
        self.snap[(key, myval)] = snapshot
        token = (key, myval)
        for r in reads:
            reg = self.regions.setdefault(r, [[], {}])
            reg[1][key] = max(reg[1].get(key, 0), myval)
        for w in writes:
            self.regions[w] = [[token], {}]
        return token

    def set_write(self, region, tokens):
        self.regions[region] = [list(tokens), {}]

    def wait_tokens(self, eng, tokens):
        need = {}
        for t in tokens:
            self._edge(need, t)
        clk = self.clock[eng]
        waits = []
        for k, v in sorted(need.items()):
            if clk.get(k, 0) < v:
                waits.append((self.key2sem[k], v))
                psnap = self.snap.get((k, v))
                if psnap:
                    for kk, vv in psnap.items():
                        if clk.get(kk, 0) < vv:
                            clk[kk] = vv
                clk[k] = max(clk.get(k, 0), v)
        self.streams[eng].append((None, waits, None, None, 0, 0))

    def fence(self):
        toks = [(k, v) for k, v in self.cnt.items()]
        for e in ENGINES:
            self.wait_tokens(e, toks)

    def n_ops(self):
        return {e: len(s) for e, s in self.streams.items()}

    def n_waits(self):
        return sum(len(w) for s in self.streams.values() for st in s
                   for w in [st[1]])

    def emit(self, block):
        waited = set()
        for s in self.streams.values():
            for st in s:
                for sem_v in st[1]:
                    pass
        waited_vals = {}
        for s in self.streams.values():
            for st in s:
                for semobj, v in st[1]:
                    waited_vals.setdefault(id(semobj), set()).add(v)

        def mk(name):
            ekey, esem = self.esem[name]
            wset = waited_vals.get(id(esem), set())

            def body(engine):
                pending = 0
                for fn, waits, key, sem, delta, myval in self.streams[name]:
                    for s, v in waits:
                        engine.wait_ge(s, v)
                    if fn is None:
                        continue
                    instr = fn(engine)
                    if key == ekey:
                        pending += delta
                        if myval in wset:
                            instr.then_inc(sem, pending)
                            pending = 0
                    else:
                        instr.then_inc(sem, delta)
            return body
        block.sync(mk("sync"))
        block.tensor(mk("tensor"))
        block.vector(mk("vector"))
        block.scalar(mk("scalar"))
        block.gpsimd(mk("gpsimd"))


def build_program(n_experts=NE, do_attn=True, do_f3=True, verbose=False, dbg=False):
    nc = bass.Bass("TRN2", target_bir_lowering=False, debug=False)

    x_in = nc.dram_tensor("x", [D, TLOC], F32, kind="ExternalInput").ap()
    qw = nc.dram_tensor("qw", [D, D], F32, kind="ExternalInput").ap()
    kw = nc.dram_tensor("kw", [D, D], F32, kind="ExternalInput").ap()
    vw = nc.dram_tensor("vw", [D, D], F32, kind="ExternalInput").ap()
    ow = nc.dram_tensor("ow", [D, D], F32, kind="ExternalInput").ap()
    biases = nc.dram_tensor("biases", [128, 64], F32, kind="ExternalInput").ap()
    gate = nc.dram_tensor("gate", [D, NE], F32, kind="ExternalInput").ap()
    gup = nc.dram_tensor("gup", [NE, D, 2 * FH], F32, kind="ExternalInput").ap()
    down = nc.dram_tensor("down", [NE, NI, D], F32, kind="ExternalInput").ap()
    maskq = nc.dram_tensor("maskq", [128, 2048], F32, kind="ExternalInput").ap()
    y_out = nc.dram_tensor("y", [D, TLOC], F32, kind="ExternalOutput").ap()

    dbgq = dbgk = dbgv = None
    if dbg:
        dbgq = nc.dram_tensor("dbgq", [D, TLOC], F32, kind="ExternalOutput").ap()
        dbgx = nc.dram_tensor("dbgx", [D, TLOC], F32, kind="ExternalOutput").ap()
        dbgk = nc.dram_tensor("dbgk", [4 * D, TLOC], F32, kind="ExternalOutput").ap()
        dbgv = nc.dram_tensor("dbgv", [4 * TLOC, D // 2], F32, kind="ExternalOutput").ap()
    x_scr = nc.dram_tensor("x_scr", [D, TLOC], F32, kind="Internal").ap()
    qscr = nc.dram_tensor("qscr", [D, TLOC], BF16, kind="Internal").ap()
    kag_in = nc.dram_tensor("kag_in", [D, TLOC], BF16, kind="Internal").ap()
    vag_in4 = [nc.dram_tensor(f"vag_in{g}", [128, 4 * 2048], BF16,
                              kind="Internal").ap() for g in range(4)]
    kag = nc.dram_tensor("kag", [4 * D, TLOC], BF16, kind="Internal").ap()
    vag4 = [nc.dram_tensor(f"vag{g}", [4 * 128, 4 * 2048], BF16,
                           kind="Internal").ap() for g in range(4)]

    from contextlib import ExitStack
    es = ExitStack()
    sb = lambda n, s, d: es.enter_context(nc.sbuf_tensor(n, s, d))
    psa = lambda n, s, d: es.enter_context(nc.psum_tensor(n, s, d))
    with es:
        wslot = [sb(f"wslot{i}", [128, 12288], BF16) for i in range(2)]
        xch = [sb(f"xch{i}", [128, TP], F32) for i in range(2)]
        xsq = sb("xsq", [128, TP], F32)
        wfms = sb("wfms", [NE, TP], BF16)
        wrow = sb("wrow", [1, TP], BF16)
        w_b = sb("w_b", [128, TP], BF16)
        gateb = sb("gateb", [128, NDC * NE], BF16)
        qkvob = sb("qkvob", [128, 80], F32)
        stA = sb("stA", [65, TLOC], F32)
        srow2 = sb("srow2", [1, 512], F32)
        srowB = sb("srowB", [1, 512], F32)
        onest = sb("onest", [65, 128], F32)
        rsc = sb("rsc", [NE, TP], F32)
        ltm = sb("ltm", [128, 8 * NE], F32)
        wtm = sb("wtm", [128, 8 * NE], F32)
        lcur = sb("lcur", [128, NE], F32)
        eqb = sb("eqb", [128, NE], F32)
        sel32 = sb("sel32", [128, NE], F32)
        m1b = sb("m1b", [128, 2], F32)
        ident = sb("ident", [128, 128], BF16)
        identf = sb("identf", [128, 128], F32)
        onesr = sb("onesr", [1, 128], F32)
        onesrb = sb("onesrb", [1, 128], BF16)
        onesc = sb("onesc", [128, 1], F32)
        onescb = sb("onescb", [128, 1], BF16)

        pgt = [psa(f"pg{i}", [128, 512], F32) for i in range(4)]
        pdt = [psa(f"pd{i}", [128, 512], F32) for i in range(2)]
        pbt = psa("pb", [128, 512], F32)
        pxt = psa("px", [128, 512], F32)
        pgu = [t[:, :] for t in pgt]
        NGU = 4
        pdn = [t[:, :] for t in pdt]
        pbc = pbt[:, :]
        pex2 = pxt[:, :]
        pdn4 = [pdn[0], pdn[1], pbc, pex2]
        pdn4regs = ["pdn0", "pdn1", "pbc", "pex2"]
        ptr = pxt[:, :].bitcast(BF16)
        sspairs = [(pgu[0], pgu[1]), (pgu[2], pgu[3])]
        pstRegs = [("pgu0", "pgu1"), ("pgu2", "pgu3")]
        pso = pdn[0]
        psl = pdt[1][0:1, 0:256]
        plinv = pbt[:, 0:256]

        S_ = Sched(nc, es)
        cck = S_.add_sem("cck", es)
        ccv = S_.add_sem("ccv", es)
        ccv2 = S_.add_sem("ccv2", es)

        def cast_load(dst_sb, src_ap, nchunk, width, region, per=4):
            toks = []
            for a in range(0, nchunk, per):
                k = min(per, nchunk - a)
                src = src_ap[a * 128:(a + k) * 128, :].rearrange(
                    "(dc p) f -> p dc f", p=128)
                dst = dst_sb[:, a * width:(a + k) * width].rearrange(
                    "p (dc f) -> p dc f", f=width)
                toks.append(S_.op("gpsimd", lambda g, src=src, dst=dst:
                                  g.dma_start(out=dst, in_=src), dma=True,
                                  writes=[region] if a == 0 else ()))
            S_.set_write(region, toks)
            return toks

        S_.op("vector", lambda v: v.memset(xsq[:, 0:128], 1.0), writes=["xsq"])
        S_.op("gpsimd", lambda g: g.affine_select(
            out=identf[:, :], in_=xsq[:, 0:128], pattern=[[1, 128]],
            compare_op=ALU.is_equal, fill=0.0, base=0, channel_multiplier=-1),
            reads=["xsq"], writes=["identf"])
        S_.op("vector", lambda v: v.tensor_copy(out=ident[:, :], in_=identf[:, :]),
              reads=["identf"], writes=["ident"])
        S_.op("vector", lambda v: v.memset(onesr[:, :], 1.0), writes=["onesr"])
        S_.op("vector", lambda v: v.memset(onesrb[:, :], 1.0), writes=["onesrb"])
        S_.op("vector", lambda v: v.memset(onest[0:1, :], 1.0 / D), writes=["onest0"])
        S_.op("vector", lambda v: v.memset(onest[64:65, :], 1.0), writes=["onest1"])
        S_.op("vector", lambda v: v.memset(onesc[:, :], 1.0), writes=["onesc"])
        S_.op("vector", lambda v: v.memset(onescb[:, :], 1.0), writes=["onescb"])
        cast_load(gateb, gate, NDC, NE, "gateb", per=16)
        S_.op("sync", lambda s: s.dma_start(out=qkvob[:, 0:64], in_=biases),
              writes=["qkvob"], dma=True)
        S_.op("vector", lambda v: v.tensor_scalar(
            out=qkvob[:, 64:80], in0=qkvob[:, 0:16], scalar1=float(SCALE),
            scalar2=None, op0=ALU.mult), reads=["qkvob"], writes=["qkvob2"])

        def emit_stats_finalize(p):
            for n in range(2):
                sl = slice(p * TP + n * 512, p * TP + (n + 1) * 512)
                S_.op("vector", lambda v, n=n, sl=sl: v.tensor_copy(
                    out=stA[0:1, sl], in_=pgu[n][0:1, 0:512]),
                    reads=[f"pgu{n}"], writes=[f"stA0_{p}_{n}"])
                S_.op("vector", lambda v, n=n, sl=sl: v.tensor_tensor(
                    out=srow2[:, :], in0=stA[0:1, sl], in1=stA[0:1, sl],
                    op=ALU.mult), reads=[f"stA0_{p}_{n}"], writes=["srow2"])
                S_.op("vector", lambda v, n=n: v.scalar_tensor_tensor(
                    out=srowB[:, :], in0=srow2[:, :], scalar=-1.0 / D,
                    in1=pdn[n][0:1, 0:512], op0=ALU.mult, op1=ALU.add),
                    reads=["srow2", f"pdn{n}"], writes=["srowB"])
                S_.op("vector", lambda v: v.tensor_scalar(
                    out=srow2[:, :], in0=srowB[:, :], scalar1=1.0 / D,
                    scalar2=float(EPS), op0=ALU.mult, op1=ALU.add),
                    reads=["srowB"], writes=["srow2"])
                S_.op("scalar", lambda s: s.activation(
                    out=srowB[:, :], in_=srow2[:, :], func=AF.Sqrt),
                    reads=["srow2"], writes=["srowB"])
                S_.op("vector", lambda v, sl=sl: v.reciprocal(
                    out=stA[64:65, sl], in_=srowB[:, :]),
                    reads=["srowB"], writes=[f"stA1_{p}_{n}"])

        def emit_stats_stream(src_ap, src_regs, p):
            sx_tok = [None, None]
            sq_tok = [None, None]
            for dc in range(NDC):
                buf = xch[dc % 2]
                breg = f"xch{dc % 2}"
                S_.op("sync", lambda s, dc=dc, buf=buf: s.dma_start(
                    out=buf[:, :], in_=src_ap[dc * 128:(dc + 1) * 128,
                                              p * TP:(p + 1) * TP]),
                    reads=[src_regs[dc]] if src_regs else (),
                    writes=[breg], dma=True)
                S_.op("scalar", lambda s, buf=buf: s.activation(
                    out=xsq[:, :], in_=buf[:, :], func=AF.Square),
                    reads=[breg], writes=["xsq"])
                for n in range(2):
                    sx_tok[n] = S_.op(
                        "tensor", lambda t_, buf=buf, n=n, dc=dc: t_.matmul(
                            pgu[n][0:1, 0:512], onesc[:, :],
                            buf[:, n * 512:(n + 1) * 512],
                            start=(dc == 0), stop=(dc == NDC - 1)),
                        reads=["onesc", breg],
                        writes=[f"pgu{n}"] if dc == 0 else [])
                    sq_tok[n] = S_.op(
                        "tensor", lambda t_, n=n, dc=dc: t_.matmul(
                            pdn[n][0:1, 0:512], onesc[:, :],
                            xsq[:, n * 512:(n + 1) * 512],
                            start=(dc == 0), stop=(dc == NDC - 1)),
                        reads=["onesc", "xsq"],
                        writes=[f"pdn{n}"] if dc == 0 else [])
            for n in range(2):
                S_.set_write(f"pgu{n}", [sx_tok[n]])
                S_.set_write(f"pdn{n}", [sq_tok[n]])
            emit_stats_finalize(p)

        def emit_ln_normalize(src_ap, src_regs, toff, xnb_t, xnb_regs,
                              colw, coff):
            p = toff // TP
            for n in range(2):
                sl = slice(toff + n * 512, toff + (n + 1) * 512)
                S_.op("tensor", lambda t, n=n, sl=sl: t.matmul(
                    pgu[n], onest[0:1, :],
                    stA[0:1, sl], start=True, stop=True),
                    reads=["onest0", f"stA0_{p}_{n}"], writes=[f"pgu{n}"])
                S_.op("tensor", lambda t, n=n, sl=sl: t.matmul(
                    pgu[2 + n], onest[64:65, :],
                    stA[64:65, sl], start=True, stop=True),
                    reads=["onest1", f"stA1_{p}_{n}"], writes=[f"pgu{2 + n}"])
            for dc in range(NDC):
                buf = xch[dc % 2]
                breg = f"xch{dc % 2}"
                S_.op("sync", lambda s, dc=dc, buf=buf: s.dma_start(
                    out=buf[:, :], in_=src_ap[dc * 128:(dc + 1) * 128,
                                              toff:toff + TP]),
                    reads=[src_regs[dc]] if src_regs else (),
                    writes=[breg], dma=True)
                for n in range(2):
                    S_.op("vector", lambda v, buf=buf, n=n: v.tensor_tensor(
                        out=xsq[:, n * 512:(n + 1) * 512],
                        in0=buf[:, n * 512:(n + 1) * 512], in1=pgu[n],
                        op=ALU.subtract),
                        reads=[breg, f"pgu{n}"], writes=["xsq"])
                    S_.op("vector",
                          lambda v, dc=dc, n=n, xnb_t=xnb_t: v.tensor_tensor(
                              out=xnb_t[:, dc * colw + coff + n * 512:
                                        dc * colw + coff + (n + 1) * 512],
                              in0=xsq[:, n * 512:(n + 1) * 512],
                              in1=pgu[2 + n], op=ALU.mult),
                          reads=["xsq", f"pgu{2 + n}"], writes=[xnb_regs[dc]])

        def emit_router(xnb_t, xnb_regs, colw, coff):
            for n in range(2):
                dstp = pbc if n == 0 else pex2
                dreg = "pbc" if n == 0 else "pex2"
                tok = None
                for dc in range(NDC):
                    tok = S_.op(
                        "tensor", lambda t, dc=dc, n=n, dstp=dstp: t.matmul(
                            dstp[0:NE, 0:512],
                            gateb[:, dc * NE:(dc + 1) * NE],
                            xnb_t[:, dc * colw + coff + n * 512:
                                  dc * colw + coff + (n + 1) * 512],
                            start=(dc == 0), stop=(dc == NDC - 1)),
                        reads=["gateb", xnb_regs[dc]],
                        writes=[dreg] if dc == 0 else [])
                S_.set_write(dreg, [tok])
                S_.op("vector", lambda v, n=n, dstp=dstp: v.tensor_copy(
                    out=rsc[:, n * 512:(n + 1) * 512], in_=dstp[0:NE, 0:512]),
                    reads=[dreg], writes=["rsc"])
            for tt in range(8):
                dstp = pbc if tt % 2 == 0 else pex2
                dreg = "pbc" if tt % 2 == 0 else "pex2"
                S_.op("tensor", lambda t, tt=tt, dstp=dstp: t.transpose(
                    dstp[:, 0:NE], rsc[0:NE, tt * 128:(tt + 1) * 128],
                    identf[0:NE, 0:NE]),
                    reads=["rsc", "identf"], writes=[dreg])
                S_.op("vector", lambda v, tt=tt, dstp=dstp: v.tensor_copy(
                    out=ltm[:, tt * NE:(tt + 1) * NE], in_=dstp[:, 0:NE]),
                    reads=[dreg], writes=[f"ltm{tt}"])
            for tt in range(8):
                lslice = ltm[:, tt * NE:(tt + 1) * NE]
                S_.op("vector", lambda v, lslice=lslice: v.tensor_copy(
                    out=lcur[:, :], in_=lslice),
                    reads=[f"ltm{tt}"], writes=["lcur"])
                for it in range(4):
                    S_.op("vector", lambda v: v.reduce_max(
                        m1b[:, 0:1], lcur[:, :], axis=AX.X),
                        reads=["lcur"], writes=["m1max"])
                    if it == 0:
                        S_.op("vector", lambda v: v.tensor_scalar(
                            out=m1b[:, 1:2], in0=m1b[:, 0:1], scalar1=-1.0,
                            scalar2=None, op0=ALU.mult),
                            reads=["m1max"], writes=["m1neg"])
                    S_.op("vector", lambda v: v.tensor_scalar(
                        out=eqb[:, :], in0=lcur[:, :], scalar1=m1b[:, 0:1],
                        scalar2=None, op0=ALU.is_equal),
                        reads=["lcur", "m1max"], writes=["eqb"])
                    if it == 0:
                        S_.op("vector", lambda v: v.tensor_copy(
                            out=sel32[:, :], in_=eqb[:, :]),
                            reads=["eqb"], writes=["sel32"])
                    else:
                        S_.op("vector", lambda v: v.tensor_tensor(
                            out=sel32[:, :], in0=sel32[:, :], in1=eqb[:, :],
                            op=ALU.add), reads=["sel32", "eqb"],
                            writes=["sel32"])
                    if it < 3:
                        S_.op("vector", lambda v: v.scalar_tensor_tensor(
                            out=lcur[:, :], in0=eqb[:, :], scalar=-BIG,
                            in1=lcur[:, :], op0=ALU.mult, op1=ALU.add),
                            reads=["eqb", "lcur"], writes=["lcur"])
                S_.op("scalar", lambda s, lslice=lslice: s.activation(
                    out=eqb[:, :], in_=lslice, func=AF.Exp, bias=m1b[:, 1:2]),
                    reads=[f"ltm{tt}", "m1neg"], writes=["eqb"])
                S_.op("vector", lambda v: v.tensor_tensor(
                    out=eqb[:, :], in0=eqb[:, :], in1=sel32[:, :],
                    op=ALU.mult), reads=["eqb", "sel32"], writes=["eqb"])
                S_.op("vector", lambda v: v.reduce_sum(
                    m1b[:, 0:1], eqb[:, :], axis=AX.X),
                    reads=["eqb"], writes=["m1max"])
                S_.op("vector", lambda v: v.reciprocal(
                    out=m1b[:, 1:2], in_=m1b[:, 0:1]),
                    reads=["m1max"], writes=["m1neg"])
                S_.op("vector", lambda v, tt=tt: v.tensor_scalar(
                    out=wtm[:, tt * NE:(tt + 1) * NE], in0=eqb[:, :],
                    scalar1=m1b[:, 1:2], scalar2=0.5, op0=ALU.mult,
                    op1=ALU.mult), reads=["eqb", "m1neg"],
                    writes=[f"wtm{tt}"])
            wtoks = []
            for tt in range(8):
                dstp = pbc if tt % 2 == 0 else pex2
                dreg = "pbc" if tt % 2 == 0 else "pex2"
                S_.op("tensor", lambda t, tt=tt, dstp=dstp: t.transpose(
                    dstp[0:NE, 0:128], wtm[:, tt * NE:(tt + 1) * NE],
                    identf[:, :]),
                    reads=[f"wtm{tt}", "identf"], writes=[dreg])
                wtoks.append(S_.op("vector", lambda v, tt=tt, dstp=dstp:
                                   v.tensor_copy(
                                       out=wfms[:, tt * 128:(tt + 1) * 128],
                                       in_=dstp[0:NE, 0:128]),
                                   reads=[dreg],
                                   writes=["wfms"] if tt == 0 else []))
            S_.set_write("wfms", wtoks)

        stage_counter = [0]
        consume_counter = [0]

        def emit_stage_dma(desc):
            sid = stage_counter[0]
            stage_counter[0] += 1
            slot = wslot[sid % 2]
            reg = f"wslot{sid % 2}"
            kind = desc[0]
            if kind == "g":
                cast_load(slot, gup[desc[1], :, 0:FH], NDC, FH, reg)
            elif kind == "u":
                cast_load(slot, gup[desc[1], :, FH:2 * FH], NDC, FH, reg)
            elif kind == "dn":
                cast_load(slot, down[desc[1]], NIC, D, reg, per=3)
            elif kind == "proj":
                mat, qtr = desc[1], desc[2]
                cast_load(slot, mat[:, qtr * 512:(qtr + 1) * 512],
                          NDC, 512, reg)
            return sid

        def emit_moe_pass(p, src_ap, src_regs, dst_ap, dst_regs, with_stats,
                          prefetch):
            emit_ln_normalize(src_ap, src_regs, p * TP, xnb, xnb_regs, TP, 0)
            emit_router(xnb, xnb_regs, TP, 0)
            for e in range(n_experts):
                S_.op("sync", lambda s, e=e: s.dma_start(
                    out=wrow[:, :], in_=wfms[e:e + 1, :]),
                    reads=["wfms"], writes=["wrow"], dma=True)
                for n in range(2):
                    dstp = pbc if n == 0 else pex2
                    dreg = "pbc" if n == 0 else "pex2"
                    S_.op("tensor", lambda t, n=n, dstp=dstp: t.matmul(
                        dstp, onesrb[:, :],
                        wrow[0:1, n * 512:(n + 1) * 512],
                        start=True, stop=True),
                        reads=["onesrb", "wrow"], writes=[dreg])
                    S_.op("vector", lambda v, n=n, dstp=dstp: v.tensor_copy(
                        out=w_b[:, n * 512:(n + 1) * 512], in_=dstp),
                        reads=[dreg], writes=[f"w_b{n}"])
                for kind in ("g", "u", "dn"):
                    sid = consume_counter[0]
                    consume_counter[0] += 1
                    slot = wslot[sid % 2]
                    sreg = f"wslot{sid % 2}"
                    if kind in ("g", "u"):
                        for f in range(NFC):
                            for n in range(2):
                                par = (f * 2 + n) % NGU
                                preg = f"pgu{par}"
                                tok = None
                                for dc in range(NDC):
                                    tok = S_.op(
                                        "tensor",
                                        lambda t, f=f, n=n, dc=dc, par=par,
                                        slot=slot: t.matmul(
                                            pgu[par],
                                            slot[:, dc * FH + f * 128:
                                                 dc * FH + (f + 1) * 128],
                                            xnb[:, dc * TP + n * 512:
                                                dc * TP + (n + 1) * 512],
                                            start=(dc == 0),
                                            stop=(dc == NDC - 1)),
                                        reads=[sreg, xnb_regs[dc]],
                                        writes=[preg] if dc == 0 else [])
                                S_.set_write(preg, [tok])
                                if kind == "g":
                                    S_.op("scalar",
                                          lambda s, f=f, n=n, par=par:
                                          s.activation(
                                              out=silu_g[:,
                                                         f * TP + n * 512:
                                                         f * TP + (n + 1) * 512],
                                              in_=pgu[par],
                                              func=AF.Silu),
                                          reads=[preg],
                                          writes=[f"silu{f}_{n}"])
                                else:
                                    uwv = xsq[:, :].bitcast(BF16)
                                    S_.op("vector",
                                          lambda v, n=n, par=par, uwv=uwv:
                                          v.tensor_tensor(
                                              out=uwv[:, n * 512:(n + 1) * 512],
                                              in0=pgu[par],
                                              in1=w_b[:, n * 512:(n + 1) * 512],
                                              op=ALU.mult),
                                          reads=[preg, f"w_b{n}"],
                                          writes=["xsq"])
                                    S_.op("vector",
                                          lambda v, f=f, n=n, uwv=uwv:
                                          v.tensor_tensor(
                                              out=gated[:, f * TP + n * 512:
                                                        f * TP + (n + 1) * 512],
                                              in0=uwv[:, n * 512:(n + 1) * 512],
                                              in1=silu_g[:, f * TP + n * 512:
                                                         f * TP + (n + 1) * 512],
                                              op=ALU.mult),
                                          reads=["xsq", f"silu{f}_{n}"],
                                          writes=[f"gated{f}_{n}"])
                    else:
                        for do in range(NDC):
                            for n in range(2):
                                par = (do * 2 + n) % 4
                                preg = pdn4regs[par]
                                tok = None
                                for ic in range(NIC):
                                    tok = S_.op(
                                        "tensor",
                                        lambda t, do=do, n=n, ic=ic, par=par,
                                        slot=slot: t.matmul(
                                            pdn4[par],
                                            slot[:, ic * D + do * 128:
                                                 ic * D + (do + 1) * 128],
                                            gated[:, ic * TP + n * 512:
                                                  ic * TP + (n + 1) * 512],
                                            start=(ic == 0),
                                            stop=(ic == NIC - 1)),
                                        reads=[sreg, f"gated{ic}_{n}"],
                                        writes=[preg] if ic == 0 else [])
                                S_.set_write(preg, [tok])
                                dst = acc[:, do * TP + n * 512:
                                          do * TP + (n + 1) * 512]
                                if e == 0:
                                    S_.op("vector",
                                          lambda v, dst=dst, par=par:
                                          v.tensor_copy(out=dst,
                                                        in_=pdn4[par]),
                                          reads=[preg],
                                          writes=[f"acc{do}_{n}"])
                                else:
                                    S_.op("vector",
                                          lambda v, dst=dst, par=par:
                                          v.tensor_tensor(
                                              out=dst, in0=pdn4[par],
                                              in1=dst, op=ALU.add),
                                          reads=[preg, f"acc{do}_{n}"],
                                          writes=[f"acc{do}_{n}"])
                    if prefetch:
                        emit_stage_dma(prefetch.pop(0))
            sx_tok = [None, None]
            sq_tok = [None, None]
            for dc in range(NDC):
                buf = xch[dc % 2]
                breg = f"xch{dc % 2}"
                S_.op("sync", lambda s, dc=dc, buf=buf: s.dma_start(
                    out=buf[:, :], in_=src_ap[dc * 128:(dc + 1) * 128,
                                              p * TP:(p + 1) * TP]),
                    reads=[src_regs[dc]] if src_regs else (),
                    writes=[breg], dma=True)
                S_.op("vector", lambda v, dc=dc, buf=buf: v.tensor_tensor(
                    out=xsq[:, :], in0=buf[:, :],
                    in1=acc[:, dc * TP:(dc + 1) * TP], op=ALU.add),
                    reads=[breg] + [f"acc{dc}_{n}" for n in range(2)],
                    writes=["xsq"])
                S_.op("sync", lambda s, dc=dc: s.dma_start(
                    out=dst_ap[dc * 128:(dc + 1) * 128, p * TP:(p + 1) * TP],
                    in_=xsq[:, :]),
                    reads=["xsq"], writes=[dst_regs[dc]], dma=True)
                if with_stats:
                    S_.op("scalar", lambda s, buf=buf: s.activation(
                        out=buf[:, :], in_=xsq[:, :], func=AF.Square),
                        reads=["xsq"], writes=[breg])
                    for n in range(2):
                        sx_tok[n] = S_.op(
                            "tensor", lambda t_, n=n, dc=dc: t_.matmul(
                                pgu[n][0:1, 0:512], onesc[:, :],
                                xsq[:, n * 512:(n + 1) * 512],
                                start=(dc == 0), stop=(dc == NDC - 1)),
                            reads=["onesc", "xsq"],
                            writes=[f"pgu{n}"] if dc == 0 else [])
                        sq_tok[n] = S_.op(
                            "tensor", lambda t_, buf=buf, n=n, dc=dc: t_.matmul(
                                pdn[n][0:1, 0:512], onesc[:, :],
                                buf[:, n * 512:(n + 1) * 512],
                                start=(dc == 0), stop=(dc == NDC - 1)),
                            reads=["onesc", breg],
                            writes=[f"pdn{n}"] if dc == 0 else [])
            if with_stats:
                for n in range(2):
                    S_.set_write(f"pgu{n}", [sx_tok[n]])
                    S_.set_write(f"pdn{n}", [sq_tok[n]])
                emit_stats_finalize(p)

        xscr_regs = [[f"xscr{dc}_{p}" for dc in range(NDC)] for p in range(2)]

        acc = sb("acc", [128, NDC * TP], F32)
        silu_g = sb("silu_g", [128, NFC * TP], BF16)
        gated = sb("gated", [128, NFC * TP], BF16)
        xnb = sb("xnb", [128, NDC * TP], BF16)
        xnb_regs = [f"xnb{dc}" for dc in range(NDC)]

        emit_stats_stream(x_in, None, 0)
        emit_stats_stream(x_in, None, 1)

        def moe_prefetch_list():
            return [(k, e) for e in range(n_experts) for k in ("g", "u", "dn")]

        pf = moe_prefetch_list() + moe_prefetch_list()
        emit_stage_dma(pf.pop(0))
        emit_stage_dma(pf.pop(0))
        emit_moe_pass(0, x_in, None, x_scr, xscr_regs[0], True, pf)
        emit_moe_pass(1, x_in, None, x_scr, xscr_regs[1], True, pf)
        assert not pf

        if do_attn:
            S_.fence()
            xnbA_ap = acc[:, :].bitcast(BF16)
            xnbA_regs = [f"xnbA{dc}" for dc in range(NDC)]
            for hp in range(2):
                emit_ln_normalize(x_scr, xscr_regs[hp], hp * TP,
                                  xnbA_ap, xnbA_regs, 2048, hp * TP)
            if dbg:
                xtoks = []
                for dc in range(NDC):
                    xtoks.append(S_.op("gpsimd",
                        lambda g, dc=dc: g.dma_start(
                            out=dbgx[dc * 128:(dc + 1) * 128, :],
                            in_=xnbA_ap[:, dc * 2048:(dc + 1) * 2048]),
                        reads=[xnbA_regs[dc]], writes=["dbgx"] if dc == 0 else (),
                        dma=True))
                S_.set_write("dbgx", xtoks)
                S_.op("vector", lambda v: v.engine_nop(), reads=["dbgx"])
            qstag = gated[:, 0:512]
            kstag = gated[:, 512:1024]
            vstag = gated[:, 1024:1536]
            vstg2 = gated[:, 1536:2048]
            dumped = [False]
            for mi, (mat, dst_dram, dname, bcol, scl) in (
                    (1, (kw, kag_in, "kagin", 16, None)),
                    (2, (vw, vag_in4, "vagin", 32, None)),
                    (0, (qw, qscr, "qscr", 64, None))):
                for qtr in range(4):
                    emit_stage_dma(("proj", mat, qtr))
                    sid = stage_counter[0] - 1
                    consume_counter[0] = stage_counter[0]
                    slot = wslot[sid % 2]
                    sreg = f"wslot{sid % 2}"
                    if dbg and not dumped[0]:
                        dumped[0] = True
                        for a4 in range(4):
                            S_.op("gpsimd", lambda g, a4=a4, slot=slot:
                                  g.dma_start(
                                      out=dbgx[a4 * 128:(a4 + 1) * 128, :],
                                      in_=slot[:, a4 * 2048:(a4 + 1) * 2048]),
                                  reads=[sreg],
                                  writes=["dbgslot"] if a4 == 0 else (),
                                  dma=True)
                        S_.op("vector", lambda v: v.engine_nop(),
                              reads=["dbgslot"])
                    for hh in range(4):
                        h = qtr * 4 + hh
                        for n in range(4):
                            par = (hh * 4 + n) % NGU
                            preg = f"pgu{par}"
                            tok = None
                            for dc in range(NDC):
                                tok = S_.op(
                                    "tensor",
                                    lambda t, hh=hh, n=n, dc=dc, par=par,
                                    slot=slot: t.matmul(
                                        pgu[par],
                                        slot[:, dc * 512 + hh * 128:
                                             dc * 512 + (hh + 1) * 128],
                                        xnbA_ap[:, dc * 2048 + n * 512:
                                                dc * 2048 + (n + 1) * 512],
                                        start=(dc == 0), stop=(dc == NDC - 1)),
                                    reads=[sreg, xnbA_regs[dc]],
                                    writes=[preg] if dc == 0 else [])
                            S_.set_write(preg, [tok])
                            if mi == 0:
                                if dbg and h == 0 and n == 0:
                                    S_.op("gpsimd", lambda g, par=par:
                                          g.dma_start(
                                              out=dbgx[512:640, 0:512].bitcast(F32) if False else dbgx[512:640, 0:512],
                                              in_=pgu[par])
                                          if False else g.dma_start(
                                              out=dbgx[640:768, 0:512],
                                              in_=xnbA_ap[:, 0:512]),
                                          reads=[preg, xnbA_regs[0]],
                                          writes=["dbgrhs"], dma=True)
                                S_.op("scalar",
                                      lambda s, par=par, h=h: s.activation(
                                          out=qstag, in_=pgu[par],
                                          func=AF.Identity, scale=float(SCALE),
                                          bias=qkvob[:, 64 + h:65 + h]),
                                      reads=[preg, "qkvob2"],
                                      writes=["qstag"])
                                if dbg and h == 0 and n == 0:
                                    S_.op("gpsimd", lambda g: g.dma_start(
                                          out=dbgx[512:640, 0:512],
                                          in_=qstag),
                                          reads=["qstag"],
                                          writes=["dbgqs"], dma=True)
                                    S_.op("vector", lambda v: v.engine_nop(),
                                          reads=["dbgqs", "dbgrhs"])
                                S_.op("sync", lambda s, h=h, n=n: s.dma_start(
                                    out=qscr[h * 128:(h + 1) * 128,
                                             n * 512:(n + 1) * 512],
                                    in_=qstag),
                                    reads=["qstag"],
                                    writes=[f"qscr{h}_{n}"], dma=True)
                            elif mi == 1:
                                S_.op("scalar",
                                      lambda s, par=par, h=h: s.activation(
                                          out=kstag, in_=pgu[par],
                                          func=AF.Identity,
                                          bias=qkvob[:, 16 + h:17 + h]),
                                      reads=[preg, "qkvob"],
                                      writes=["kstag"])
                                S_.op("sync", lambda s, h=h, n=n: s.dma_start(
                                    out=kag_in[h * 128:(h + 1) * 128,
                                               n * 512:(n + 1) * 512],
                                    in_=kstag),
                                    reads=["kstag"],
                                    writes=[f"kagin{h}_{n}"], dma=True)
                                if h == H - 1 and n == 3:
                                    kin_regs = [f"kagin{hx}_{nx}"
                                                for hx in range(H)
                                                for nx in range(4)]
                                    S_.op("gpsimd",
                                          lambda g: g.collective_compute(
                                              "AllGather", ALU.bypass,
                                              replica_groups=[[0, 1, 2, 3],
                                                              [4, 5, 6, 7]],
                                              ins=[kag_in], outs=[kag]),
                                          reads=kin_regs, writes=["kag"],
                                          sem_override=cck)
                            else:
                                S_.op("scalar",
                                      lambda s, par=par, h=h: s.activation(
                                          out=vstag, in_=pgu[par],
                                          func=AF.Identity,
                                          bias=qkvob[:, 32 + h:33 + h]),
                                      reads=[preg, "qkvob"],
                                      writes=["vstag"])
                                for t4 in range(4):
                                    S_.op("tensor",
                                          lambda t, t4=t4:
                                          t.transpose(
                                              ptr[:, 0:128],
                                              vstag[:, t4 * 128:
                                                    (t4 + 1) * 128],
                                              ident[:, :]),
                                          reads=["vstag", "ident"],
                                          writes=["pex2"])
                                    S_.op("vector",
                                          lambda v, t4=t4:
                                          v.tensor_copy(
                                              out=vstg2[:, t4 * 128:
                                                        (t4 + 1) * 128],
                                              in_=ptr[:, 0:128]),
                                          reads=["pex2"],
                                          writes=[f"vstg2_{t4}"])
                                    tg = n * 4 + t4
                                    gid = h // 4
                                    hin = h % 4
                                    S_.op("sync",
                                          lambda s, gid=gid, hin=hin, tg=tg,
                                          t4=t4:
                                          s.dma_start(
                                              out=vag_in4[gid][
                                                  :, hin * 2048 + tg * 128:
                                                  hin * 2048 + (tg + 1) * 128],
                                              in_=vstg2[:, t4 * 128:
                                                        (t4 + 1) * 128]),
                                          reads=[f"vstg2_{t4}"],
                                          writes=[f"vagin{tg}_{h}"],
                                          dma=True)
                                    if hin == 3 and tg == 15:
                                        vin = [f"vagin{t_}_{h_}"
                                               for t_ in range(16)
                                               for h_ in range(gid * 4,
                                                               gid * 4 + 4)]
                                        S_.op("gpsimd",
                                              lambda g, gid=gid:
                                              g.collective_compute(
                                                  "AllGather", ALU.bypass,
                                                  replica_groups=[[0, 1, 2, 3],
                                                                  [4, 5, 6, 7]],
                                                  ins=[vag_in4[gid]],
                                                  outs=[vag4[gid]]),
                                              reads=vin,
                                              writes=[f"vag4_{gid}"],
                                              sem_override=(ccv if gid < 2
                                                            else ccv2))

            kbuf = [xnb[:, 0:8192], xnb[:, 8192:16384]]
            vbuf = [acc[:, :].bitcast(BF16)[:, 0:8192],
                    acc[:, :].bitcast(BF16)[:, 8192:16384]]
            attout = acc[:, :].bitcast(BF16)[:, 16384:20480]
            qtl = [gated[:, 2048 + i * 256: 2048 + (i + 1) * 256]
                   for i in range(2)]
            expB = [gated[:, 2560 + i * 1024: 2560 + (i + 1) * 1024]
                    for i in range(2)]
            tmpE = gated[:, 4608:5120]
            esF = [gated[:, 5120 + i * 512:5120 + (i + 1) * 512].bitcast(F32)
                   for i in range(2)]
            maskt = silu_g[:, 0:2048]
            S_.op("gpsimd", lambda g: g.dma_start(
                out=maskt, in_=maskq), writes=["maskt"], dma=True)

            ss_ctr = [0]
            for qb in range(8):
                nk = qb + 1
                nu = 8 * nk
                for h in range(H):
                    hpar = h % 2
                    kb = kbuf[hpar]
                    vb = vbuf[hpar]
                    kreg = f"kbuf{hpar}"
                    vreg = f"vbuf{hpar}"
                    ktoks = []
                    vtoks = []
                    for qr in range(4):
                        ktoks.append(S_.op(
                            "sync", lambda s, qr=qr, h=h, nk=nk, kb=kb:
                            s.dma_start(
                                out=kb[:, qr * nk * 256:(qr + 1) * nk * 256],
                                in_=kag[qr * D + h * 128: qr * D + (h + 1) * 128,
                                        0:nk * 256]),
                            reads=["kag"],
                            writes=[kreg] if qr == 0 else (), dma=True))
                        gid = h // 4
                        hin = h % 4
                        vsrc = vag4[gid][qr * 128:(qr + 1) * 128,
                                         hin * 2048: hin * 2048 + nk * 256]
                        vdst = vb[:, qr * nk * 256:(qr + 1) * nk * 256]
                        vtoks.append(S_.op(
                            "sync", lambda s, vsrc=vsrc, vdst=vdst:
                            s.dma_start(out=vdst, in_=vsrc),
                            reads=[f"vag4_{gid}"],
                            writes=[vreg] if qr == 0 else (), dma=True))
                    S_.set_write(kreg, ktoks)
                    S_.set_write(vreg, vtoks)
                    qpar = h % 2
                    S_.op("sync", lambda s, h=h, qb=qb, qpar=qpar: s.dma_start(
                        out=qtl[qpar], in_=qscr[h * 128:(h + 1) * 128,
                                                qb * 256:(qb + 1) * 256]),
                        reads=[f"qscr{h}_{qb // 2}"],
                        writes=[f"qtl{qpar}"], dma=True)

                    l_last = o_last = None
                    u_seen = 0
                    for qr in range(4):
                        blk = 2 * nk
                        sizes = [4] * (blk // 4) + ([2] if blk % 4 else [])
                        v0 = 0
                        for bi, bs in enumerate(sizes):
                            sspar = ss_ctr[0] % 2
                            ss_ctr[0] += 1
                            pair = sspairs[sspar]
                            regs = pstRegs[sspar]
                            eb = expB[sspar]
                            ereg = f"expB{sspar}"
                            toks = [None, None]
                            for i in range(bs):
                                u = qr * blk + v0 + i
                                hi = i // 2
                                toks[hi] = S_.op(
                                    "tensor",
                                    lambda t, i=i, u=u, hi=hi, pair=pair,
                                    kb=kb, qpar=qpar:
                                    t.matmul(
                                        pair[hi][:, (i % 2) * 256:
                                                 (i % 2 + 1) * 256],
                                        kb[:, u * 128:(u + 1) * 128],
                                        qtl[qpar], start=True, stop=True),
                                    reads=[kreg, f"qtl{qpar}"],
                                    writes=[regs[hi]] if i % 2 == 0 else [])
                            for hi in range((bs + 1) // 2):
                                S_.set_write(regs[hi], [toks[hi]])
                            final = (bi == len(sizes) - 1)
                            nh = bs // 2
                            for hi in range(nh):
                                last_half = (hi == nh - 1)
                                if final and last_half:
                                    S_.op("scalar",
                                          lambda s, pair=pair, hi=hi:
                                          s.activation(
                                              out=tmpE[:, :],
                                              in_=pair[hi][:, 0:512],
                                              func=AF.Exp),
                                          reads=[regs[hi]], writes=["tmpE"])
                                    S_.op("vector",
                                          lambda v, eb=eb, hi=hi, qr=qr:
                                          v.tensor_tensor(
                                              out=eb[:, hi * 512:
                                                     (hi + 1) * 512],
                                              in0=tmpE[:, :],
                                              in1=maskt[:, qr * 512:
                                                        (qr + 1) * 512],
                                              op=ALU.mult),
                                          reads=["tmpE", "maskt"],
                                          writes=[ereg])
                                else:
                                    S_.op("scalar",
                                          lambda s, pair=pair, eb=eb, hi=hi:
                                          s.activation(
                                              out=eb[:, hi * 512:
                                                     (hi + 1) * 512],
                                              in_=pair[hi][:, 0:512],
                                              func=AF.Exp),
                                          reads=[regs[hi]], writes=[ereg])
                            for i in range(bs):
                                u = qr * blk + v0 + i
                                st = (u_seen + i == 0)
                                sp = (u_seen + i == nu - 1)
                                l_last = S_.op(
                                    "tensor",
                                    lambda t, i=i, eb=eb, st=st, sp=sp:
                                    t.matmul(
                                        psl, onescb[:, :],
                                        eb[:, i * 256:(i + 1) * 256],
                                        start=st, stop=sp),
                                    reads=["onescb", ereg],
                                    writes=["pdn1"] if st else [])
                            for i in range(bs):
                                u = qr * blk + v0 + i
                                st = (u_seen + i == 0)
                                sp = (u_seen + i == nu - 1)
                                o_last = S_.op(
                                    "tensor",
                                    lambda t, i=i, u=u, eb=eb, vb=vb,
                                    st=st, sp=sp:
                                    t.matmul(
                                        pso[:, 0:256],
                                        vb[:, u * 128:(u + 1) * 128],
                                        eb[:, i * 256:(i + 1) * 256],
                                        start=st, stop=sp),
                                    reads=[vreg, ereg],
                                    writes=["pdn0"] if st else [])
                            u_seen += bs
                            v0 += bs
                    S_.set_write("pdn1", [l_last])
                    S_.set_write("pdn0", [o_last])
                    S_.op("vector", lambda v: v.reciprocal(
                        out=ltm[0:1, 0:256], in_=psl),
                        reads=["pdn1"], writes=["lrowA"])
                    S_.op("tensor", lambda t: t.matmul(
                        plinv[:, 0:256], onesr[:, :], ltm[0:1, 0:256],
                        start=True, stop=True),
                        reads=["onesr", "lrowA"], writes=["pbc"])
                    S_.op("vector", lambda v: v.tensor_copy(
                        out=wtm[:, 0:256], in_=plinv[:, 0:256]),
                        reads=["pbc"], writes=["linvbA"])
                    S_.op("vector", lambda v, h=h: v.tensor_tensor(
                        out=attout[:, h * 256:(h + 1) * 256],
                        in0=pso[:, 0:256], in1=wtm[:, 0:256],
                        op=ALU.mult),
                        reads=["pdn0", "linvbA"],
                        writes=[f"attout{h}"])
                for qtr in range(4):
                    emit_stage_dma(("proj", ow, qtr))
                    sid = stage_counter[0] - 1
                    consume_counter[0] = stage_counter[0]
                    slot = wslot[sid % 2]
                    sreg = f"wslot{sid % 2}"
                    for do in range(4):
                        do_g = qtr * 4 + do
                        par = do % 2
                        dstp = pbc if par == 0 else pex2
                        dreg = "pbc" if par == 0 else "pex2"
                        tok = None
                        for dc in range(NDC):
                            tok = S_.op(
                                "tensor",
                                lambda t, do=do, dc=dc, dstp=dstp, slot=slot:
                                t.matmul(
                                    dstp[:, 0:256],
                                    slot[:, dc * 512 + do * 128:
                                         dc * 512 + (do + 1) * 128],
                                    attout[:, dc * 256:(dc + 1) * 256],
                                    start=(dc == 0), stop=(dc == NDC - 1)),
                                reads=[sreg, f"attout{dc}"],
                                writes=[dreg] if dc == 0 else [])
                        S_.set_write(dreg, [tok])
                        S_.op("scalar", lambda s, dstp=dstp, do_g=do_g:
                              s.activation(
                                  out=xsq[:, 0:256], in_=dstp[:, 0:256],
                                  func=AF.Identity,
                                  bias=qkvob[:, 48 + do_g:49 + do_g]),
                              reads=[dreg, "qkvob"], writes=["xsqA"])
                        bpar = do_g % 2
                        S_.op("sync", lambda s, do_g=do_g, qb=qb, bpar=bpar:
                              s.dma_start(
                                  out=xch[bpar][:, 0:256],
                                  in_=x_scr[do_g * 128:(do_g + 1) * 128,
                                            qb * 256:(qb + 1) * 256]),
                              reads=[f"xscr{do_g}_{qb // 4}"],
                              writes=[f"xch{bpar}"], dma=True)
                        S_.op("vector", lambda v, bpar=bpar: v.tensor_tensor(
                            out=xsq[:, 256:512], in0=xsq[:, 0:256],
                            in1=xch[bpar][:, 0:256], op=ALU.add),
                            reads=["xsqA", f"xch{bpar}"], writes=["xsqB"])
                        S_.op("sync", lambda s, do_g=do_g, qb=qb: s.dma_start(
                            out=x_scr[do_g * 128:(do_g + 1) * 128,
                                      qb * 256:(qb + 1) * 256],
                            in_=xsq[:, 256:512]),
                            reads=["xsqB"],
                            writes=[f"xscr{do_g}_{qb // 4}"], dma=True)
            if do_f3:
                pf3 = moe_prefetch_list() + moe_prefetch_list()
                emit_stage_dma(pf3.pop(0))
                emit_stage_dma(pf3.pop(0))
            S_.fence()
            if dbg:
                qregs = [f"qscr{h}_{n}" for h in range(H) for n in range(4)]
                S_.op("gpsimd", lambda g: g.dma_start(out=dbgq, in_=qscr),
                      reads=qregs, writes=["dbgq"], dma=True)
                S_.op("gpsimd", lambda g: g.dma_start(out=dbgk, in_=kag),
                      reads=["kag"], writes=["dbgk"], dma=True)
                S_.op("gpsimd", lambda g: g.dma_start(
                      out=dbgv[0:512, :], in_=vag4[0]),
                      reads=["vag4_0"], writes=["dbgv"], dma=True)
                S_.wait_tokens("gpsimd", [S_.regions[r][0][0] for r in ("dbgq", "dbgk", "dbgv")])
            emit_stats_stream(x_scr, xscr_regs[0], 0)
            emit_stats_stream(x_scr, xscr_regs[1], 1)

        y_regs = [[f"y{dc}_{p}" for dc in range(NDC)] for p in range(2)]
        if do_f3:
            pf = pf3 if do_attn else None
            if pf is None:
                pf = moe_prefetch_list() + moe_prefetch_list()
                emit_stage_dma(pf.pop(0))
                emit_stage_dma(pf.pop(0))
            emit_moe_pass(0, x_scr, xscr_regs[0], y_out, y_regs[0], False, pf)
            emit_moe_pass(1, x_scr, xscr_regs[1], y_out, y_regs[1], False, pf)
            assert not pf
        else:
            for p in range(2):
                for dc in range(NDC):
                    S_.op("sync", lambda s, dc=dc, p=p: s.dma_start(
                        out=y_out[dc * 128:(dc + 1) * 128,
                                  p * TP:(p + 1) * TP],
                        in_=x_scr[dc * 128:(dc + 1) * 128,
                                  p * TP:(p + 1) * TP]),
                        reads=[f"xscr{dc}_{p}"],
                        writes=[f"y{dc}_{p}"], dma=True)

        ytoks = []
        for p in range(2):
            for dc in range(NDC):
                reg = S_.regions.get(f"y{dc}_{p}")
                if reg:
                    ytoks.extend(reg[0])
        S_.wait_tokens("sync", ytoks)

        if verbose:
            print(f"sched ops={S_.n_ops()} waits={S_.n_waits()}")
        with nc.Block() as block:
            S_.emit(block)

    return nc


def host_shard(x_full):
    out = []
    xr = np.asarray(x_full).reshape(B, 8, 4, 256, D)
    for r in range(8):
        b, j = r // 4, r % 4
        xc = xr[b, :, j].reshape(TLOC, D)
        out.append(np.ascontiguousarray(xc.T))
    return out


def host_mask(j):
    m = np.zeros((128, 2048), np.float32)
    for qr in range(4):
        for m2 in range(2):
            kk = m2 * 128 + np.arange(128)[:, None]
            qq = np.arange(256)[None, :]
            valid = (256 * qr + kk) <= (256 * j + qq)
            m[:, (qr * 2 + m2) * 256:(qr * 2 + m2 + 1) * 256] = valid
    return m


def kernel(L0_attn_x, L0_attn_q_w, L0_attn_q_b, L0_attn_k_w, L0_attn_k_b,
           L0_attn_v_w, L0_attn_v_b, L0_attn_o_w, L0_attn_o_b,
           L0_ffn_gate, L0_ffn_gate_up_proj, L0_ffn_down_proj):
    nc = build_program()

    xs = host_shard(np.asarray(L0_attn_x, dtype=np.float32))
    biases = np.stack([
        np.asarray(L0_attn_q_b, np.float32).reshape(16, 128).T,
        np.asarray(L0_attn_k_b, np.float32).reshape(16, 128).T,
        np.asarray(L0_attn_v_b, np.float32).reshape(16, 128).T,
        np.asarray(L0_attn_o_b, np.float32).reshape(16, 128).T,
    ], axis=1).reshape(128, 64)
    common = {
        "qw": np.ascontiguousarray(np.asarray(L0_attn_q_w, np.float32).T),
        "kw": np.ascontiguousarray(np.asarray(L0_attn_k_w, np.float32).T),
        "vw": np.ascontiguousarray(np.asarray(L0_attn_v_w, np.float32).T),
        "ow": np.ascontiguousarray(np.asarray(L0_attn_o_w, np.float32).T),
        "biases": np.ascontiguousarray(biases),
        "gate": np.ascontiguousarray(np.asarray(L0_ffn_gate, np.float32)),
        "gup": np.ascontiguousarray(np.asarray(L0_ffn_gate_up_proj, np.float32)),
        "down": np.ascontiguousarray(np.asarray(L0_ffn_down_proj, np.float32)),
    }
    in_maps = []
    for r in range(8):
        m = dict(common)
        m["x"] = xs[r]
        m["maskq"] = host_mask(r % 4)
        in_maps.append(m)

    res = run_bass_kernel_spmd(nc, in_maps, core_ids=list(range(8)))

    out = np.zeros((B, 8, 4, 256, D), np.float32)
    for r in range(8):
        b, j = r // 4, r % 4
        yc = np.asarray(res.results[r]["y"])
        out[b, :, j] = yc.T.reshape(8, 256, D)
    return out.reshape(B, SEQ, D)


# version 11
# speedup vs baseline: 1.0068x; 1.0068x over previous
import numpy as np
import concourse.bass as bass
import concourse.mybir as mybir
from concourse.bass_utils import run_bass_kernel_spmd

F32 = mybir.dt.float32
BF16 = mybir.dt.bfloat16
AF = mybir.ActivationFunctionType
ALU = mybir.AluOpType
AX = mybir.AxisListType

B, SEQ, D = 2, 8192, 2048
H, HD = 16, 128
SCALE = 1.0 / float(np.sqrt(HD))
NE = 32
FH = 768
NI = 768
TLOC = 2048
TP = 1024
NDC = D // 128
NFC = FH // 128
NIC = NI // 128
EPS = 1e-5
BIG = 1e30

ENGINES = ("sync", "tensor", "vector", "scalar", "gpsimd")
NDMASEM = 6


class Sched:

    def __init__(self, nc, es):
        self.nc = nc
        self.esem = {}
        self.key2sem = {}
        for e in ENGINES:
            s = es.enter_context(nc.semaphore(f"prog_{e}"))
            self.esem[e] = ("E" + e, s)
            self.key2sem["E" + e] = s
        self.dsem = {}
        for e in ENGINES:
            lst = []
            for i in range(NDMASEM):
                s = es.enter_context(nc.semaphore(f"dma_{e}_{i}"))
                key = f"D{e}{i}"
                lst.append((key, s))
                self.key2sem[key] = s
            self.dsem[e] = lst
        self.cnt = {}
        self.dmause = {e: 0 for e in ENGINES}
        self.streams = {e: [] for e in ENGINES}
        self.clock = {e: {} for e in ENGINES}
        self.snap = {}
        self.regions = {}

    def add_sem(self, name, es):
        s = es.enter_context(self.nc.semaphore(name))
        key = "X" + name
        self.key2sem[key] = s
        return (key, s)

    def _edge(self, need, token):
        k, v = token
        if need.get(k, 0) < v:
            need[k] = v

    def op(self, eng, fn, reads=(), writes=(), dma=False, extra_tokens=(),
           sem_override=None):
        need = {}
        for r in reads:
            reg = self.regions.get(r)
            if reg:
                for t in reg[0]:
                    self._edge(need, t)
        for w in writes:
            reg = self.regions.get(w)
            if reg:
                for t in reg[0]:
                    self._edge(need, t)
                for rk, rv in reg[1].items():
                    self._edge(need, (rk, rv))
        for t in extra_tokens:
            self._edge(need, t)
        if sem_override is not None:
            key, sem = sem_override
            delta = 1
            myval = self.cnt.get(key, 0) + 1
        elif dma:
            i = self.dmause[eng] % NDMASEM
            self.dmause[eng] += 1
            key, sem = self.dsem[eng][i]
            prev = self.cnt.get(key, 0)
            if prev:
                self._edge(need, (key, prev))
            delta = 16
            myval = prev + 16
        else:
            key, sem = self.esem[eng]
            delta = 1
            myval = self.cnt.get(key, 0) + 1
        clk = self.clock[eng]
        waits = []
        for k, v in sorted(need.items()):
            if clk.get(k, 0) < v:
                waits.append((self.key2sem[k], v))
                psnap = self.snap.get((k, v))
                if psnap:
                    for kk, vv in psnap.items():
                        if clk.get(kk, 0) < vv:
                            clk[kk] = vv
                if clk.get(k, 0) < v:
                    clk[k] = v
        self.cnt[key] = myval
        self.streams[eng].append((fn, waits, key, sem, delta, myval))
        snapshot = dict(clk)
        snapshot[key] = myval
        self.snap[(key, myval)] = snapshot
        token = (key, myval)
        for r in reads:
            reg = self.regions.setdefault(r, [[], {}])
            reg[1][key] = max(reg[1].get(key, 0), myval)
        for w in writes:
            self.regions[w] = [[token], {}]
        return token

    def set_write(self, region, tokens):
        self.regions[region] = [list(tokens), {}]

    def wait_tokens(self, eng, tokens):
        need = {}
        for t in tokens:
            self._edge(need, t)
        clk = self.clock[eng]
        waits = []
        for k, v in sorted(need.items()):
            if clk.get(k, 0) < v:
                waits.append((self.key2sem[k], v))
                psnap = self.snap.get((k, v))
                if psnap:
                    for kk, vv in psnap.items():
                        if clk.get(kk, 0) < vv:
                            clk[kk] = vv
                clk[k] = max(clk.get(k, 0), v)
        self.streams[eng].append((None, waits, None, None, 0, 0))

    def fence(self):
        toks = [(k, v) for k, v in self.cnt.items()]
        for e in ENGINES:
            self.wait_tokens(e, toks)

    def n_ops(self):
        return {e: len(s) for e, s in self.streams.items()}

    def n_waits(self):
        return sum(len(w) for s in self.streams.values() for st in s
                   for w in [st[1]])

    def emit(self, block):
        waited = set()
        for s in self.streams.values():
            for st in s:
                for sem_v in st[1]:
                    pass
        waited_vals = {}
        for s in self.streams.values():
            for st in s:
                for semobj, v in st[1]:
                    waited_vals.setdefault(id(semobj), set()).add(v)

        def mk(name):
            ekey, esem = self.esem[name]
            wset = waited_vals.get(id(esem), set())

            def body(engine):
                pending = 0
                for fn, waits, key, sem, delta, myval in self.streams[name]:
                    for s, v in waits:
                        engine.wait_ge(s, v)
                    if fn is None:
                        continue
                    instr = fn(engine)
                    if key == ekey:
                        pending += delta
                        if myval in wset:
                            instr.then_inc(sem, pending)
                            pending = 0
                    else:
                        instr.then_inc(sem, delta)
            return body
        block.sync(mk("sync"))
        block.tensor(mk("tensor"))
        block.vector(mk("vector"))
        block.scalar(mk("scalar"))
        block.gpsimd(mk("gpsimd"))


def build_program(n_experts=NE, do_attn=True, do_f3=True, verbose=False, dbg=False):
    nc = bass.Bass("TRN2", target_bir_lowering=False, debug=False)

    x_in = nc.dram_tensor("x", [D, TLOC], F32, kind="ExternalInput").ap()
    qw = nc.dram_tensor("qw", [D, D], F32, kind="ExternalInput").ap()
    kw = nc.dram_tensor("kw", [D, D], F32, kind="ExternalInput").ap()
    vw = nc.dram_tensor("vw", [D, D], F32, kind="ExternalInput").ap()
    ow = nc.dram_tensor("ow", [D, D], F32, kind="ExternalInput").ap()
    biases = nc.dram_tensor("biases", [128, 64], F32, kind="ExternalInput").ap()
    gate = nc.dram_tensor("gate", [D, NE], F32, kind="ExternalInput").ap()
    gup = nc.dram_tensor("gup", [NE, D, 2 * FH], F32, kind="ExternalInput").ap()
    down = nc.dram_tensor("down", [NE, NI, D], F32, kind="ExternalInput").ap()
    maskq = nc.dram_tensor("maskq", [128, 2048], F32, kind="ExternalInput").ap()
    y_out = nc.dram_tensor("y", [D, TLOC], F32, kind="ExternalOutput").ap()

    dbgq = dbgk = dbgv = None
    if dbg:
        dbgq = nc.dram_tensor("dbgq", [D, TLOC], F32, kind="ExternalOutput").ap()
        dbgx = nc.dram_tensor("dbgx", [D, TLOC], F32, kind="ExternalOutput").ap()
        dbgk = nc.dram_tensor("dbgk", [4 * D, TLOC], F32, kind="ExternalOutput").ap()
        dbgv = nc.dram_tensor("dbgv", [4 * TLOC, D // 2], F32, kind="ExternalOutput").ap()
    x_scr = nc.dram_tensor("x_scr", [D, TLOC], F32, kind="Internal").ap()
    qscr = nc.dram_tensor("qscr", [D, TLOC], BF16, kind="Internal").ap()
    kag_in = nc.dram_tensor("kag_in", [D, TLOC], BF16, kind="Internal").ap()
    vag_in4 = [nc.dram_tensor(f"vag_in{g}", [128, 4 * 2048], BF16,
                              kind="Internal").ap() for g in range(4)]
    kag = nc.dram_tensor("kag", [4 * D, TLOC], BF16, kind="Internal").ap()
    vag4 = [nc.dram_tensor(f"vag{g}", [4 * 128, 4 * 2048], BF16,
                           kind="Internal").ap() for g in range(4)]

    from contextlib import ExitStack
    es = ExitStack()
    sb = lambda n, s, d: es.enter_context(nc.sbuf_tensor(n, s, d))
    psa = lambda n, s, d: es.enter_context(nc.psum_tensor(n, s, d))
    with es:
        wslot = [sb(f"wslot{i}", [128, 12288], BF16) for i in range(2)]
        xch = [sb(f"xch{i}", [128, TP], F32) for i in range(2)]
        xsq = sb("xsq", [128, TP], F32)
        wfms = sb("wfms", [NE, TP], BF16)
        wrow = sb("wrow", [1, TP], BF16)
        w_b = sb("w_b", [128, TP], BF16)
        gateb = sb("gateb", [128, NDC * NE], BF16)
        qkvob = sb("qkvob", [128, 80], F32)
        stA = sb("stA", [65, TLOC], F32)
        srow2 = sb("srow2", [1, 512], F32)
        srowB = sb("srowB", [1, 512], F32)
        onest = sb("onest", [65, 128], F32)
        rsc = sb("rsc", [NE, TP], F32)
        ltm = sb("ltm", [128, 8 * NE], F32)
        wtm = sb("wtm", [128, 8 * NE], F32)
        lcur = sb("lcur", [128, NE], F32)
        eqb = sb("eqb", [128, NE], F32)
        sel32 = sb("sel32", [128, NE], F32)
        m1b = sb("m1b", [128, 2], F32)
        ident = sb("ident", [128, 128], BF16)
        identf = sb("identf", [128, 128], F32)
        onesr = sb("onesr", [1, 128], F32)
        onesrb = sb("onesrb", [1, 128], BF16)
        onesc = sb("onesc", [128, 1], F32)
        onescb = sb("onescb", [128, 1], BF16)

        pgt = [psa(f"pg{i}", [128, 512], F32) for i in range(4)]
        pdt = [psa(f"pd{i}", [128, 512], F32) for i in range(2)]
        pbt = psa("pb", [128, 512], F32)
        pxt = psa("px", [128, 512], F32)
        pgu = [t[:, :] for t in pgt]
        NGU = 4
        pdn = [t[:, :] for t in pdt]
        pbc = pbt[:, :]
        pex2 = pxt[:, :]
        ptr = pxt[:, :].bitcast(BF16)
        sspairs = [(pgu[0], pgu[1]), (pgu[2], pgu[3])]
        pstRegs = [("pgu0", "pgu1"), ("pgu2", "pgu3")]
        pso = pdn[0]
        psl = pdt[1][0:1, 0:256]
        plinv = pbt[:, 0:256]

        S_ = Sched(nc, es)
        cck = S_.add_sem("cck", es)
        ccv = S_.add_sem("ccv", es)
        ccv2 = S_.add_sem("ccv2", es)

        def cast_load(dst_sb, src_ap, nchunk, width, region, per=4):
            toks = []
            for a in range(0, nchunk, per):
                k = min(per, nchunk - a)
                src = src_ap[a * 128:(a + k) * 128, :].rearrange(
                    "(dc p) f -> p dc f", p=128)
                dst = dst_sb[:, a * width:(a + k) * width].rearrange(
                    "p (dc f) -> p dc f", f=width)
                toks.append(S_.op("gpsimd", lambda g, src=src, dst=dst:
                                  g.dma_start(out=dst, in_=src), dma=True,
                                  writes=[region] if a == 0 else ()))
            S_.set_write(region, toks)
            return toks

        S_.op("vector", lambda v: v.memset(xsq[:, 0:128], 1.0), writes=["xsq"])
        S_.op("gpsimd", lambda g: g.affine_select(
            out=identf[:, :], in_=xsq[:, 0:128], pattern=[[1, 128]],
            compare_op=ALU.is_equal, fill=0.0, base=0, channel_multiplier=-1),
            reads=["xsq"], writes=["identf"])
        S_.op("vector", lambda v: v.tensor_copy(out=ident[:, :], in_=identf[:, :]),
              reads=["identf"], writes=["ident"])
        S_.op("vector", lambda v: v.memset(onesr[:, :], 1.0), writes=["onesr"])
        S_.op("vector", lambda v: v.memset(onesrb[:, :], 1.0), writes=["onesrb"])
        S_.op("vector", lambda v: v.memset(onest[0:1, :], 1.0 / D), writes=["onest0"])
        S_.op("vector", lambda v: v.memset(onest[64:65, :], 1.0), writes=["onest1"])
        S_.op("vector", lambda v: v.memset(onesc[:, :], 1.0), writes=["onesc"])
        S_.op("vector", lambda v: v.memset(onescb[:, :], 1.0), writes=["onescb"])
        cast_load(gateb, gate, NDC, NE, "gateb", per=16)
        S_.op("sync", lambda s: s.dma_start(out=qkvob[:, 0:64], in_=biases),
              writes=["qkvob"], dma=True)
        S_.op("vector", lambda v: v.tensor_scalar(
            out=qkvob[:, 64:80], in0=qkvob[:, 0:16], scalar1=float(SCALE),
            scalar2=None, op0=ALU.mult), reads=["qkvob"], writes=["qkvob2"])

        def emit_stats_finalize(p):
            for n in range(2):
                sl = slice(p * TP + n * 512, p * TP + (n + 1) * 512)
                S_.op("vector", lambda v, n=n, sl=sl: v.tensor_copy(
                    out=stA[0:1, sl], in_=pgu[n][0:1, 0:512]),
                    reads=[f"pgu{n}"], writes=[f"stA0_{p}_{n}"])
                S_.op("vector", lambda v, n=n, sl=sl: v.tensor_tensor(
                    out=srow2[:, :], in0=stA[0:1, sl], in1=stA[0:1, sl],
                    op=ALU.mult), reads=[f"stA0_{p}_{n}"], writes=["srow2"])
                S_.op("vector", lambda v, n=n: v.scalar_tensor_tensor(
                    out=srowB[:, :], in0=srow2[:, :], scalar=-1.0 / D,
                    in1=pdn[n][0:1, 0:512], op0=ALU.mult, op1=ALU.add),
                    reads=["srow2", f"pdn{n}"], writes=["srowB"])
                S_.op("vector", lambda v: v.tensor_scalar(
                    out=srow2[:, :], in0=srowB[:, :], scalar1=1.0 / D,
                    scalar2=float(EPS), op0=ALU.mult, op1=ALU.add),
                    reads=["srowB"], writes=["srow2"])
                S_.op("scalar", lambda s: s.activation(
                    out=srowB[:, :], in_=srow2[:, :], func=AF.Sqrt),
                    reads=["srow2"], writes=["srowB"])
                S_.op("vector", lambda v, sl=sl: v.reciprocal(
                    out=stA[64:65, sl], in_=srowB[:, :]),
                    reads=["srowB"], writes=[f"stA1_{p}_{n}"])

        def emit_stats_stream(src_ap, src_regs, p):
            sx_tok = [None, None]
            sq_tok = [None, None]
            for dc in range(NDC):
                buf = xch[dc % 2]
                breg = f"xch{dc % 2}"
                S_.op("sync", lambda s, dc=dc, buf=buf: s.dma_start(
                    out=buf[:, :], in_=src_ap[dc * 128:(dc + 1) * 128,
                                              p * TP:(p + 1) * TP]),
                    reads=[src_regs[dc]] if src_regs else (),
                    writes=[breg], dma=True)
                S_.op("scalar", lambda s, buf=buf: s.activation(
                    out=xsq[:, :], in_=buf[:, :], func=AF.Square),
                    reads=[breg], writes=["xsq"])
                for n in range(2):
                    sx_tok[n] = S_.op(
                        "tensor", lambda t_, buf=buf, n=n, dc=dc: t_.matmul(
                            pgu[n][0:1, 0:512], onesc[:, :],
                            buf[:, n * 512:(n + 1) * 512],
                            start=(dc == 0), stop=(dc == NDC - 1)),
                        reads=["onesc", breg],
                        writes=[f"pgu{n}"] if dc == 0 else [])
                    sq_tok[n] = S_.op(
                        "tensor", lambda t_, n=n, dc=dc: t_.matmul(
                            pdn[n][0:1, 0:512], onesc[:, :],
                            xsq[:, n * 512:(n + 1) * 512],
                            start=(dc == 0), stop=(dc == NDC - 1)),
                        reads=["onesc", "xsq"],
                        writes=[f"pdn{n}"] if dc == 0 else [])
            for n in range(2):
                S_.set_write(f"pgu{n}", [sx_tok[n]])
                S_.set_write(f"pdn{n}", [sq_tok[n]])
            emit_stats_finalize(p)

        def emit_ln_normalize(src_ap, src_regs, toff, xnb_t, xnb_regs,
                              colw, coff):
            p = toff // TP
            for n in range(2):
                sl = slice(toff + n * 512, toff + (n + 1) * 512)
                S_.op("tensor", lambda t, n=n, sl=sl: t.matmul(
                    pgu[n], onest[0:1, :],
                    stA[0:1, sl], start=True, stop=True),
                    reads=["onest0", f"stA0_{p}_{n}"], writes=[f"pgu{n}"])
                S_.op("tensor", lambda t, n=n, sl=sl: t.matmul(
                    pgu[2 + n], onest[64:65, :],
                    stA[64:65, sl], start=True, stop=True),
                    reads=["onest1", f"stA1_{p}_{n}"], writes=[f"pgu{2 + n}"])
            for dc in range(NDC):
                buf = xch[dc % 2]
                breg = f"xch{dc % 2}"
                S_.op("sync", lambda s, dc=dc, buf=buf: s.dma_start(
                    out=buf[:, :], in_=src_ap[dc * 128:(dc + 1) * 128,
                                              toff:toff + TP]),
                    reads=[src_regs[dc]] if src_regs else (),
                    writes=[breg], dma=True)
                for n in range(2):
                    S_.op("vector", lambda v, buf=buf, n=n: v.tensor_tensor(
                        out=xsq[:, n * 512:(n + 1) * 512],
                        in0=buf[:, n * 512:(n + 1) * 512], in1=pgu[n],
                        op=ALU.subtract),
                        reads=[breg, f"pgu{n}"], writes=["xsq"])
                    S_.op("vector",
                          lambda v, dc=dc, n=n, xnb_t=xnb_t: v.tensor_tensor(
                              out=xnb_t[:, dc * colw + coff + n * 512:
                                        dc * colw + coff + (n + 1) * 512],
                              in0=xsq[:, n * 512:(n + 1) * 512],
                              in1=pgu[2 + n], op=ALU.mult),
                          reads=["xsq", f"pgu{2 + n}"],
                          writes=xnb_regs[dc] if isinstance(xnb_regs[dc], list)
                          else [xnb_regs[dc]])

        def emit_router(xnb_t, xnb_regs, colw, coff):
            for n in range(2):
                dstp = pbc if n == 0 else pex2
                dreg = "pbc" if n == 0 else "pex2"
                tok = None
                for dc in range(NDC):
                    tok = S_.op(
                        "tensor", lambda t, dc=dc, n=n, dstp=dstp: t.matmul(
                            dstp[0:NE, 0:512],
                            gateb[:, dc * NE:(dc + 1) * NE],
                            xnb_t[:, dc * colw + coff + n * 512:
                                  dc * colw + coff + (n + 1) * 512],
                            start=(dc == 0), stop=(dc == NDC - 1)),
                        reads=["gateb", xnb_regs[dc][0] if isinstance(xnb_regs[dc], list) else xnb_regs[dc]],
                        writes=[dreg] if dc == 0 else [])
                S_.set_write(dreg, [tok])
                S_.op("vector", lambda v, n=n, dstp=dstp: v.tensor_copy(
                    out=rsc[:, n * 512:(n + 1) * 512], in_=dstp[0:NE, 0:512]),
                    reads=[dreg], writes=["rsc"])
            for tt in range(8):
                dstp = pbc if tt % 2 == 0 else pex2
                dreg = "pbc" if tt % 2 == 0 else "pex2"
                S_.op("tensor", lambda t, tt=tt, dstp=dstp: t.transpose(
                    dstp[:, 0:NE], rsc[0:NE, tt * 128:(tt + 1) * 128],
                    identf[0:NE, 0:NE]),
                    reads=["rsc", "identf"], writes=[dreg])
                S_.op("vector", lambda v, tt=tt, dstp=dstp: v.tensor_copy(
                    out=ltm[:, tt * NE:(tt + 1) * NE], in_=dstp[:, 0:NE]),
                    reads=[dreg], writes=[f"ltm{tt}"])
            for tt in range(8):
                lslice = ltm[:, tt * NE:(tt + 1) * NE]
                S_.op("vector", lambda v, lslice=lslice: v.tensor_copy(
                    out=lcur[:, :], in_=lslice),
                    reads=[f"ltm{tt}"], writes=["lcur"])
                for it in range(4):
                    S_.op("vector", lambda v: v.reduce_max(
                        m1b[:, 0:1], lcur[:, :], axis=AX.X),
                        reads=["lcur"], writes=["m1max"])
                    if it == 0:
                        S_.op("vector", lambda v: v.tensor_scalar(
                            out=m1b[:, 1:2], in0=m1b[:, 0:1], scalar1=-1.0,
                            scalar2=None, op0=ALU.mult),
                            reads=["m1max"], writes=["m1neg"])
                    S_.op("vector", lambda v: v.tensor_scalar(
                        out=eqb[:, :], in0=lcur[:, :], scalar1=m1b[:, 0:1],
                        scalar2=None, op0=ALU.is_equal),
                        reads=["lcur", "m1max"], writes=["eqb"])
                    if it == 0:
                        S_.op("vector", lambda v: v.tensor_copy(
                            out=sel32[:, :], in_=eqb[:, :]),
                            reads=["eqb"], writes=["sel32"])
                    else:
                        S_.op("vector", lambda v: v.tensor_tensor(
                            out=sel32[:, :], in0=sel32[:, :], in1=eqb[:, :],
                            op=ALU.add), reads=["sel32", "eqb"],
                            writes=["sel32"])
                    if it < 3:
                        S_.op("vector", lambda v: v.scalar_tensor_tensor(
                            out=lcur[:, :], in0=eqb[:, :], scalar=-BIG,
                            in1=lcur[:, :], op0=ALU.mult, op1=ALU.add),
                            reads=["eqb", "lcur"], writes=["lcur"])
                S_.op("scalar", lambda s, lslice=lslice: s.activation(
                    out=eqb[:, :], in_=lslice, func=AF.Exp, bias=m1b[:, 1:2]),
                    reads=[f"ltm{tt}", "m1neg"], writes=["eqb"])
                S_.op("vector", lambda v: v.tensor_tensor(
                    out=eqb[:, :], in0=eqb[:, :], in1=sel32[:, :],
                    op=ALU.mult), reads=["eqb", "sel32"], writes=["eqb"])
                S_.op("vector", lambda v: v.reduce_sum(
                    m1b[:, 0:1], eqb[:, :], axis=AX.X),
                    reads=["eqb"], writes=["m1max"])
                S_.op("vector", lambda v: v.reciprocal(
                    out=m1b[:, 1:2], in_=m1b[:, 0:1]),
                    reads=["m1max"], writes=["m1neg"])
                S_.op("vector", lambda v, tt=tt: v.tensor_scalar(
                    out=wtm[:, tt * NE:(tt + 1) * NE], in0=eqb[:, :],
                    scalar1=m1b[:, 1:2], scalar2=0.5, op0=ALU.mult,
                    op1=ALU.mult), reads=["eqb", "m1neg"],
                    writes=[f"wtm{tt}"])
            wtoks = []
            for tt in range(8):
                dstp = pbc if tt % 2 == 0 else pex2
                dreg = "pbc" if tt % 2 == 0 else "pex2"
                S_.op("tensor", lambda t, tt=tt, dstp=dstp: t.transpose(
                    dstp[0:NE, 0:128], wtm[:, tt * NE:(tt + 1) * NE],
                    identf[:, :]),
                    reads=[f"wtm{tt}", "identf"], writes=[dreg])
                wtoks.append(S_.op("vector", lambda v, tt=tt, dstp=dstp:
                                   v.tensor_copy(
                                       out=wfms[:, tt * 128:(tt + 1) * 128],
                                       in_=dstp[0:NE, 0:128]),
                                   reads=[dreg],
                                   writes=["wfms"] if tt == 0 else []))
            S_.set_write("wfms", wtoks)

        stage_counter = [0]
        consume_counter = [0]

        def emit_stage_dma(desc):
            sid = stage_counter[0]
            stage_counter[0] += 1
            slot = wslot[sid % 2]
            reg = f"wslot{sid % 2}"
            kind = desc[0]
            if kind == "g":
                cast_load(slot, gup[desc[1], :, 0:FH], NDC, FH, reg)
            elif kind == "u":
                cast_load(slot, gup[desc[1], :, FH:2 * FH], NDC, FH, reg)
            elif kind == "dn":
                cast_load(slot, down[desc[1]], NIC, D, reg, per=3)
            elif kind == "proj":
                mat, qtr = desc[1], desc[2]
                cast_load(slot, mat[:, qtr * 512:(qtr + 1) * 512],
                          NDC, 512, reg)
            return sid

        def emit_moe_pass(p, src_ap, src_regs, dst_ap, dst_regs, with_stats,
                          prefetch):
            emit_ln_normalize(src_ap, src_regs, p * TP, xnb, xnb_regs, TP, 0)
            emit_router(xnb, xnb_regs, TP, 0)
            for e in range(n_experts):
                S_.op("sync", lambda s, e=e: s.dma_start(
                    out=wrow[:, :], in_=wfms[e:e + 1, :]),
                    reads=["wfms"], writes=["wrow"], dma=True)
                for n in range(2):
                    dstp = pbc if n == 0 else pex2
                    dreg = "pbc" if n == 0 else "pex2"
                    S_.op("tensor", lambda t, n=n, dstp=dstp: t.matmul(
                        dstp, onesrb[:, :],
                        wrow[0:1, n * 512:(n + 1) * 512],
                        start=True, stop=True),
                        reads=["onesrb", "wrow"], writes=[dreg])
                    S_.op("vector", lambda v, n=n, dstp=dstp: v.tensor_copy(
                        out=w_b[:, n * 512:(n + 1) * 512], in_=dstp),
                        reads=[dreg], writes=[f"w_b{n}"])
                for kind in ("g", "u", "dn"):
                    sid = consume_counter[0]
                    consume_counter[0] += 1
                    slot = wslot[sid % 2]
                    sreg = f"wslot{sid % 2}"
                    if kind in ("g", "u"):
                        for f in range(NFC):
                            for n in range(2):
                                par = (f * 2 + n) % NGU
                                preg = f"pgu{par}"
                                tok = None
                                for dc in range(NDC):
                                    tok = S_.op(
                                        "tensor",
                                        lambda t, f=f, n=n, dc=dc, par=par,
                                        slot=slot: t.matmul(
                                            pgu[par],
                                            slot[:, dc * FH + f * 128:
                                                 dc * FH + (f + 1) * 128],
                                            xnb[:, dc * TP + n * 512:
                                                dc * TP + (n + 1) * 512],
                                            start=(dc == 0),
                                            stop=(dc == NDC - 1)),
                                        reads=[sreg, xnb_regs[dc][0] if isinstance(xnb_regs[dc], list) else xnb_regs[dc]],
                                        writes=[preg] if dc == 0 else [])
                                S_.set_write(preg, [tok])
                                if kind == "g":
                                    S_.op("scalar",
                                          lambda s, f=f, n=n, par=par:
                                          s.activation(
                                              out=silu_g[:,
                                                         f * TP + n * 512:
                                                         f * TP + (n + 1) * 512],
                                              in_=pgu[par],
                                              func=AF.Silu),
                                          reads=[preg],
                                          writes=[f"silu{f}_{n}"])
                                else:
                                    uwv = xsq[:, :].bitcast(BF16)
                                    S_.op("vector",
                                          lambda v, n=n, par=par, uwv=uwv:
                                          v.tensor_tensor(
                                              out=uwv[:, n * 512:(n + 1) * 512],
                                              in0=pgu[par],
                                              in1=w_b[:, n * 512:(n + 1) * 512],
                                              op=ALU.mult),
                                          reads=[preg, f"w_b{n}"],
                                          writes=["xsq"])
                                    S_.op("vector",
                                          lambda v, f=f, n=n, uwv=uwv:
                                          v.tensor_tensor(
                                              out=gated[:, f * TP + n * 512:
                                                        f * TP + (n + 1) * 512],
                                              in0=uwv[:, n * 512:(n + 1) * 512],
                                              in1=silu_g[:, f * TP + n * 512:
                                                         f * TP + (n + 1) * 512],
                                              op=ALU.mult),
                                          reads=["xsq", f"silu{f}_{n}"],
                                          writes=[f"gated{f}_{n}"])
                    else:
                        for do in range(NDC):
                            for n in range(2):
                                par = (do * 2 + n) % 2
                                preg = f"pdn{par}"
                                tok = None
                                for ic in range(NIC):
                                    tok = S_.op(
                                        "tensor",
                                        lambda t, do=do, n=n, ic=ic, par=par,
                                        slot=slot: t.matmul(
                                            pdn[par],
                                            slot[:, ic * D + do * 128:
                                                 ic * D + (do + 1) * 128],
                                            gated[:, ic * TP + n * 512:
                                                  ic * TP + (n + 1) * 512],
                                            start=(ic == 0),
                                            stop=(ic == NIC - 1)),
                                        reads=[sreg, f"gated{ic}_{n}"],
                                        writes=[preg] if ic == 0 else [])
                                S_.set_write(preg, [tok])
                                dst = acc[:, do * TP + n * 512:
                                          do * TP + (n + 1) * 512]
                                if e == 0:
                                    S_.op("vector",
                                          lambda v, dst=dst, par=par:
                                          v.tensor_copy(out=dst,
                                                        in_=pdn[par]),
                                          reads=[preg],
                                          writes=[f"acc{do}_{n}"])
                                else:
                                    S_.op("vector",
                                          lambda v, dst=dst, par=par:
                                          v.tensor_tensor(
                                              out=dst, in0=pdn[par],
                                              in1=dst, op=ALU.add),
                                          reads=[preg, f"acc{do}_{n}"],
                                          writes=[f"acc{do}_{n}"])
                    if prefetch:
                        emit_stage_dma(prefetch.pop(0))
            sx_tok = [None, None]
            sq_tok = [None, None]
            for dc in range(NDC):
                buf = xch[dc % 2]
                breg = f"xch{dc % 2}"
                S_.op("sync", lambda s, dc=dc, buf=buf: s.dma_start(
                    out=buf[:, :], in_=src_ap[dc * 128:(dc + 1) * 128,
                                              p * TP:(p + 1) * TP]),
                    reads=[src_regs[dc]] if src_regs else (),
                    writes=[breg], dma=True)
                S_.op("vector", lambda v, dc=dc, buf=buf: v.tensor_tensor(
                    out=xsq[:, :], in0=buf[:, :],
                    in1=acc[:, dc * TP:(dc + 1) * TP], op=ALU.add),
                    reads=[breg] + [f"acc{dc}_{n}" for n in range(2)],
                    writes=["xsq"])
                S_.op("sync", lambda s, dc=dc: s.dma_start(
                    out=dst_ap[dc * 128:(dc + 1) * 128, p * TP:(p + 1) * TP],
                    in_=xsq[:, :]),
                    reads=["xsq"], writes=[dst_regs[dc]], dma=True)
                if with_stats:
                    S_.op("scalar", lambda s, buf=buf: s.activation(
                        out=buf[:, :], in_=xsq[:, :], func=AF.Square),
                        reads=["xsq"], writes=[breg])
                    for n in range(2):
                        sx_tok[n] = S_.op(
                            "tensor", lambda t_, n=n, dc=dc: t_.matmul(
                                pgu[n][0:1, 0:512], onesc[:, :],
                                xsq[:, n * 512:(n + 1) * 512],
                                start=(dc == 0), stop=(dc == NDC - 1)),
                            reads=["onesc", "xsq"],
                            writes=[f"pgu{n}"] if dc == 0 else [])
                        sq_tok[n] = S_.op(
                            "tensor", lambda t_, buf=buf, n=n, dc=dc: t_.matmul(
                                pdn[n][0:1, 0:512], onesc[:, :],
                                buf[:, n * 512:(n + 1) * 512],
                                start=(dc == 0), stop=(dc == NDC - 1)),
                            reads=["onesc", breg],
                            writes=[f"pdn{n}"] if dc == 0 else [])
            if with_stats:
                for n in range(2):
                    S_.set_write(f"pgu{n}", [sx_tok[n]])
                    S_.set_write(f"pdn{n}", [sq_tok[n]])
                emit_stats_finalize(p)

        xscr_regs = [[f"xscr{dc}_{p}" for dc in range(NDC)] for p in range(2)]

        acc = sb("acc", [128, NDC * TP], F32)
        silu_g = sb("silu_g", [128, NFC * TP], BF16)
        gated = sb("gated", [128, NFC * TP], BF16)
        xnb = sb("xnb", [128, NDC * TP], BF16)
        xnb_regs = [f"xnb{dc}" for dc in range(NDC)]

        emit_stats_stream(x_in, None, 0)
        emit_stats_stream(x_in, None, 1)

        def moe_prefetch_list():
            return [(k, e) for e in range(n_experts) for k in ("g", "u", "dn")]

        pf = moe_prefetch_list() + moe_prefetch_list()
        emit_stage_dma(pf.pop(0))
        emit_stage_dma(pf.pop(0))
        emit_moe_pass(0, x_in, None, x_scr, xscr_regs[0], True, pf)
        emit_moe_pass(1, x_in, None, x_scr, xscr_regs[1], True, pf)
        assert not pf

        if do_attn:
            xnbA_ap = acc[:, :].bitcast(BF16)
            xnbA_regs = [[f"xnbA{dc}", f"acc{dc}_0", f"acc{dc}_1"]
                         for dc in range(NDC)]
            for hp in range(2):
                emit_ln_normalize(x_scr, xscr_regs[hp], hp * TP,
                                  xnbA_ap, xnbA_regs, 2048, hp * TP)
            if dbg:
                xtoks = []
                for dc in range(NDC):
                    xtoks.append(S_.op("gpsimd",
                        lambda g, dc=dc: g.dma_start(
                            out=dbgx[dc * 128:(dc + 1) * 128, :],
                            in_=xnbA_ap[:, dc * 2048:(dc + 1) * 2048]),
                        reads=[xnbA_regs[dc]], writes=["dbgx"] if dc == 0 else (),
                        dma=True))
                S_.set_write("dbgx", xtoks)
                S_.op("vector", lambda v: v.engine_nop(), reads=["dbgx"])
            qstag = gated[:, 0:512]
            kstag = gated[:, 512:1024]
            vstag = gated[:, 1024:1536]
            vstg2 = gated[:, 1536:2048]
            dumped = [False]
            for mi, (mat, dst_dram, dname, bcol, scl) in (
                    (1, (kw, kag_in, "kagin", 16, None)),
                    (2, (vw, vag_in4, "vagin", 32, None)),
                    (0, (qw, qscr, "qscr", 64, None))):
                for qtr in range(4):
                    emit_stage_dma(("proj", mat, qtr))
                    sid = stage_counter[0] - 1
                    consume_counter[0] = stage_counter[0]
                    slot = wslot[sid % 2]
                    sreg = f"wslot{sid % 2}"
                    if dbg and not dumped[0]:
                        dumped[0] = True
                        for a4 in range(4):
                            S_.op("gpsimd", lambda g, a4=a4, slot=slot:
                                  g.dma_start(
                                      out=dbgx[a4 * 128:(a4 + 1) * 128, :],
                                      in_=slot[:, a4 * 2048:(a4 + 1) * 2048]),
                                  reads=[sreg],
                                  writes=["dbgslot"] if a4 == 0 else (),
                                  dma=True)
                        S_.op("vector", lambda v: v.engine_nop(),
                              reads=["dbgslot"])
                    for hh in range(4):
                        h = qtr * 4 + hh
                        for n in range(4):
                            par = (hh * 4 + n) % NGU
                            preg = f"pgu{par}"
                            tok = None
                            for dc in range(NDC):
                                tok = S_.op(
                                    "tensor",
                                    lambda t, hh=hh, n=n, dc=dc, par=par,
                                    slot=slot: t.matmul(
                                        pgu[par],
                                        slot[:, dc * 512 + hh * 128:
                                             dc * 512 + (hh + 1) * 128],
                                        xnbA_ap[:, dc * 2048 + n * 512:
                                                dc * 2048 + (n + 1) * 512],
                                        start=(dc == 0), stop=(dc == NDC - 1)),
                                    reads=[sreg, xnbA_regs[dc][0]],
                                    writes=[preg] if dc == 0 else [])
                            S_.set_write(preg, [tok])
                            if mi == 0:
                                if dbg and h == 0 and n == 0:
                                    S_.op("gpsimd", lambda g, par=par:
                                          g.dma_start(
                                              out=dbgx[512:640, 0:512].bitcast(F32) if False else dbgx[512:640, 0:512],
                                              in_=pgu[par])
                                          if False else g.dma_start(
                                              out=dbgx[640:768, 0:512],
                                              in_=xnbA_ap[:, 0:512]),
                                          reads=[preg, xnbA_regs[0]],
                                          writes=["dbgrhs"], dma=True)
                                S_.op("scalar",
                                      lambda s, par=par, h=h: s.activation(
                                          out=qstag, in_=pgu[par],
                                          func=AF.Identity, scale=float(SCALE),
                                          bias=qkvob[:, 64 + h:65 + h]),
                                      reads=[preg, "qkvob2"],
                                      writes=["qstag", "gated0_0"])
                                if dbg and h == 0 and n == 0:
                                    S_.op("gpsimd", lambda g: g.dma_start(
                                          out=dbgx[512:640, 0:512],
                                          in_=qstag),
                                          reads=["qstag"],
                                          writes=["dbgqs"], dma=True)
                                    S_.op("vector", lambda v: v.engine_nop(),
                                          reads=["dbgqs", "dbgrhs"])
                                S_.op("sync", lambda s, h=h, n=n: s.dma_start(
                                    out=qscr[h * 128:(h + 1) * 128,
                                             n * 512:(n + 1) * 512],
                                    in_=qstag),
                                    reads=["qstag"],
                                    writes=[f"qscr{h}_{n}"], dma=True)
                            elif mi == 1:
                                S_.op("scalar",
                                      lambda s, par=par, h=h: s.activation(
                                          out=kstag, in_=pgu[par],
                                          func=AF.Identity,
                                          bias=qkvob[:, 16 + h:17 + h]),
                                      reads=[preg, "qkvob"],
                                      writes=["kstag", "gated0_1"])
                                S_.op("sync", lambda s, h=h, n=n: s.dma_start(
                                    out=kag_in[h * 128:(h + 1) * 128,
                                               n * 512:(n + 1) * 512],
                                    in_=kstag),
                                    reads=["kstag"],
                                    writes=[f"kagin{h}_{n}"], dma=True)
                                if h == H - 1 and n == 3:
                                    kin_regs = [f"kagin{hx}_{nx}"
                                                for hx in range(H)
                                                for nx in range(4)]
                                    S_.op("gpsimd",
                                          lambda g: g.collective_compute(
                                              "AllGather", ALU.bypass,
                                              replica_groups=[[0, 1, 2, 3],
                                                              [4, 5, 6, 7]],
                                              ins=[kag_in], outs=[kag]),
                                          reads=kin_regs, writes=["kag"],
                                          sem_override=cck)
                            else:
                                S_.op("scalar",
                                      lambda s, par=par, h=h: s.activation(
                                          out=vstag, in_=pgu[par],
                                          func=AF.Identity,
                                          bias=qkvob[:, 32 + h:33 + h]),
                                      reads=[preg, "qkvob"],
                                      writes=["vstag", "gated1_0"])
                                for t4 in range(4):
                                    S_.op("tensor",
                                          lambda t, t4=t4:
                                          t.transpose(
                                              ptr[:, 0:128],
                                              vstag[:, t4 * 128:
                                                    (t4 + 1) * 128],
                                              ident[:, :]),
                                          reads=["vstag", "ident"],
                                          writes=["pex2"])
                                    S_.op("vector",
                                          lambda v, t4=t4:
                                          v.tensor_copy(
                                              out=vstg2[:, t4 * 128:
                                                        (t4 + 1) * 128],
                                              in_=ptr[:, 0:128]),
                                          reads=["pex2"],
                                          writes=[f"vstg2_{t4}"])
                                    tg = n * 4 + t4
                                    gid = h // 4
                                    hin = h % 4
                                    S_.op("sync",
                                          lambda s, gid=gid, hin=hin, tg=tg,
                                          t4=t4:
                                          s.dma_start(
                                              out=vag_in4[gid][
                                                  :, hin * 2048 + tg * 128:
                                                  hin * 2048 + (tg + 1) * 128],
                                              in_=vstg2[:, t4 * 128:
                                                        (t4 + 1) * 128]),
                                          reads=[f"vstg2_{t4}"],
                                          writes=[f"vagin{tg}_{h}"],
                                          dma=True)
                                    if hin == 3 and tg == 15:
                                        vin = [f"vagin{t_}_{h_}"
                                               for t_ in range(16)
                                               for h_ in range(gid * 4,
                                                               gid * 4 + 4)]
                                        S_.op("gpsimd",
                                              lambda g, gid=gid:
                                              g.collective_compute(
                                                  "AllGather", ALU.bypass,
                                                  replica_groups=[[0, 1, 2, 3],
                                                                  [4, 5, 6, 7]],
                                                  ins=[vag_in4[gid]],
                                                  outs=[vag4[gid]]),
                                              reads=vin,
                                              writes=[f"vag4_{gid}"],
                                              sem_override=(ccv if gid < 2
                                                            else ccv2))

            kbuf = [xnb[:, 0:8192], xnb[:, 8192:16384]]
            vbuf = [acc[:, :].bitcast(BF16)[:, 0:8192],
                    acc[:, :].bitcast(BF16)[:, 8192:16384]]
            attout = acc[:, :].bitcast(BF16)[:, 16384:20480]
            qtl = [gated[:, 2048 + i * 256: 2048 + (i + 1) * 256]
                   for i in range(2)]
            expB = [gated[:, 2560 + i * 1024: 2560 + (i + 1) * 1024]
                    for i in range(2)]
            tmpE = gated[:, 4608:5120]
            esF = [gated[:, 5120 + i * 512:5120 + (i + 1) * 512].bitcast(F32)
                   for i in range(2)]
            maskt = silu_g[:, 0:2048]
            S_.op("gpsimd", lambda g: g.dma_start(
                out=maskt, in_=maskq),
                writes=["maskt", "silu0_0", "silu0_1", "silu1_0", "silu1_1"],
                dma=True)

            ss_ctr = [0]
            for qb in range(8):
                nk = qb + 1
                nu = 8 * nk
                for h in range(H):
                    hpar = h % 2
                    kb = kbuf[hpar]
                    vb = vbuf[hpar]
                    kreg = f"kbuf{hpar}"
                    vreg = f"vbuf{hpar}"
                    ktoks = []
                    vtoks = []
                    for qr in range(4):
                        ktoks.append(S_.op(
                            "sync", lambda s, qr=qr, h=h, nk=nk, kb=kb:
                            s.dma_start(
                                out=kb[:, qr * nk * 256:(qr + 1) * nk * 256],
                                in_=kag[qr * D + h * 128: qr * D + (h + 1) * 128,
                                        0:nk * 256]),
                            reads=["kag"],
                            writes=([kreg] + [f"xnb{d_}" for d_ in
                                    range(hpar * 8, hpar * 8 + 8)])
                            if qr == 0 else (), dma=True))
                        gid = h // 4
                        hin = h % 4
                        vsrc = vag4[gid][qr * 128:(qr + 1) * 128,
                                         hin * 2048: hin * 2048 + nk * 256]
                        vdst = vb[:, qr * nk * 256:(qr + 1) * nk * 256]
                        vtoks.append(S_.op(
                            "sync", lambda s, vsrc=vsrc, vdst=vdst:
                            s.dma_start(out=vdst, in_=vsrc),
                            reads=[f"vag4_{gid}"],
                            writes=([vreg] + [f"acc{d_}_{n_}" for d_ in
                                    range(hpar * 4, hpar * 4 + 4)
                                    for n_ in range(2)])
                            if qr == 0 else (), dma=True))
                    S_.set_write(kreg, ktoks)
                    S_.set_write(vreg, vtoks)
                    qpar = h % 2
                    S_.op("sync", lambda s, h=h, qb=qb, qpar=qpar: s.dma_start(
                        out=qtl[qpar], in_=qscr[h * 128:(h + 1) * 128,
                                                qb * 256:(qb + 1) * 256]),
                        reads=[f"qscr{h}_{qb // 2}"],
                        writes=[f"qtl{qpar}", "gated2_0"], dma=True)

                    l_last = o_last = None
                    u_seen = 0
                    for qr in range(4):
                        blk = 2 * nk
                        sizes = [4] * (blk // 4) + ([2] if blk % 4 else [])
                        v0 = 0
                        for bi, bs in enumerate(sizes):
                            sspar = ss_ctr[0] % 2
                            ss_ctr[0] += 1
                            pair = sspairs[sspar]
                            regs = pstRegs[sspar]
                            eb = expB[sspar]
                            ereg = f"expB{sspar}"
                            toks = [None, None]
                            for i in range(bs):
                                u = qr * blk + v0 + i
                                hi = i // 2
                                toks[hi] = S_.op(
                                    "tensor",
                                    lambda t, i=i, u=u, hi=hi, pair=pair,
                                    kb=kb, qpar=qpar:
                                    t.matmul(
                                        pair[hi][:, (i % 2) * 256:
                                                 (i % 2 + 1) * 256],
                                        kb[:, u * 128:(u + 1) * 128],
                                        qtl[qpar], start=True, stop=True),
                                    reads=[kreg, f"qtl{qpar}"],
                                    writes=[regs[hi]] if i % 2 == 0 else [])
                            for hi in range((bs + 1) // 2):
                                S_.set_write(regs[hi], [toks[hi]])
                            final = (bi == len(sizes) - 1)
                            nh = bs // 2
                            for hi in range(nh):
                                last_half = (hi == nh - 1)
                                if final and last_half:
                                    S_.op("scalar",
                                          lambda s, pair=pair, hi=hi:
                                          s.activation(
                                              out=tmpE[:, :],
                                              in_=pair[hi][:, 0:512],
                                              func=AF.Exp),
                                          reads=[regs[hi]],
                                          writes=["tmpE", "gated4_1"])
                                    S_.op("vector",
                                          lambda v, eb=eb, hi=hi, qr=qr:
                                          v.tensor_tensor(
                                              out=eb[:, hi * 512:
                                                     (hi + 1) * 512],
                                              in0=tmpE[:, :],
                                              in1=maskt[:, qr * 512:
                                                        (qr + 1) * 512],
                                              op=ALU.mult),
                                          reads=["tmpE", "maskt"],
                                          writes=[ereg, "gated2_1",
                                                  "gated3_0", "gated3_1",
                                                  "gated4_0"])
                                else:
                                    S_.op("scalar",
                                          lambda s, pair=pair, eb=eb, hi=hi:
                                          s.activation(
                                              out=eb[:, hi * 512:
                                                     (hi + 1) * 512],
                                              in_=pair[hi][:, 0:512],
                                              func=AF.Exp),
                                          reads=[regs[hi]], writes=[ereg])
                            for i in range(bs):
                                u = qr * blk + v0 + i
                                st = (u_seen + i == 0)
                                sp = (u_seen + i == nu - 1)
                                l_last = S_.op(
                                    "tensor",
                                    lambda t, i=i, eb=eb, st=st, sp=sp:
                                    t.matmul(
                                        psl, onescb[:, :],
                                        eb[:, i * 256:(i + 1) * 256],
                                        start=st, stop=sp),
                                    reads=["onescb", ereg],
                                    writes=["pdn1"] if st else [])
                            for i in range(bs):
                                u = qr * blk + v0 + i
                                st = (u_seen + i == 0)
                                sp = (u_seen + i == nu - 1)
                                o_last = S_.op(
                                    "tensor",
                                    lambda t, i=i, u=u, eb=eb, vb=vb,
                                    st=st, sp=sp:
                                    t.matmul(
                                        pso[:, 0:256],
                                        vb[:, u * 128:(u + 1) * 128],
                                        eb[:, i * 256:(i + 1) * 256],
                                        start=st, stop=sp),
                                    reads=[vreg, ereg],
                                    writes=["pdn0"] if st else [])
                            u_seen += bs
                            v0 += bs
                    S_.set_write("pdn1", [l_last])
                    S_.set_write("pdn0", [o_last])
                    S_.op("vector", lambda v: v.reciprocal(
                        out=ltm[0:1, 0:256], in_=psl),
                        reads=["pdn1"], writes=["lrowA"])
                    S_.op("tensor", lambda t: t.matmul(
                        plinv[:, 0:256], onesr[:, :], ltm[0:1, 0:256],
                        start=True, stop=True),
                        reads=["onesr", "lrowA"], writes=["pbc"])
                    S_.op("vector", lambda v: v.tensor_copy(
                        out=wtm[:, 0:256], in_=plinv[:, 0:256]),
                        reads=["pbc"], writes=["linvbA"])
                    S_.op("vector", lambda v, h=h: v.tensor_tensor(
                        out=attout[:, h * 256:(h + 1) * 256],
                        in0=pso[:, 0:256], in1=wtm[:, 0:256],
                        op=ALU.mult),
                        reads=["pdn0", "linvbA"],
                        writes=[f"attout{h}", "acc8_0", "acc8_1",
                                "acc9_0", "acc9_1"])
                for qtr in range(4):
                    emit_stage_dma(("proj", ow, qtr))
                    sid = stage_counter[0] - 1
                    consume_counter[0] = stage_counter[0]
                    slot = wslot[sid % 2]
                    sreg = f"wslot{sid % 2}"
                    for do in range(4):
                        do_g = qtr * 4 + do
                        par = do % 2
                        dstp = pbc if par == 0 else pex2
                        dreg = "pbc" if par == 0 else "pex2"
                        tok = None
                        for dc in range(NDC):
                            tok = S_.op(
                                "tensor",
                                lambda t, do=do, dc=dc, dstp=dstp, slot=slot:
                                t.matmul(
                                    dstp[:, 0:256],
                                    slot[:, dc * 512 + do * 128:
                                         dc * 512 + (do + 1) * 128],
                                    attout[:, dc * 256:(dc + 1) * 256],
                                    start=(dc == 0), stop=(dc == NDC - 1)),
                                reads=[sreg, f"attout{dc}"],
                                writes=[dreg] if dc == 0 else [])
                        S_.set_write(dreg, [tok])
                        S_.op("scalar", lambda s, dstp=dstp, do_g=do_g:
                              s.activation(
                                  out=xsq[:, 0:256], in_=dstp[:, 0:256],
                                  func=AF.Identity,
                                  bias=qkvob[:, 48 + do_g:49 + do_g]),
                              reads=[dreg, "qkvob"], writes=["xsqA"])
                        bpar = do_g % 2
                        S_.op("sync", lambda s, do_g=do_g, qb=qb, bpar=bpar:
                              s.dma_start(
                                  out=xch[bpar][:, 0:256],
                                  in_=x_scr[do_g * 128:(do_g + 1) * 128,
                                            qb * 256:(qb + 1) * 256]),
                              reads=[f"xscr{do_g}_{qb // 4}"],
                              writes=[f"xch{bpar}"], dma=True)
                        S_.op("vector", lambda v, bpar=bpar: v.tensor_tensor(
                            out=xsq[:, 256:512], in0=xsq[:, 0:256],
                            in1=xch[bpar][:, 0:256], op=ALU.add),
                            reads=["xsqA", f"xch{bpar}"], writes=["xsqB"])
                        S_.op("sync", lambda s, do_g=do_g, qb=qb: s.dma_start(
                            out=x_scr[do_g * 128:(do_g + 1) * 128,
                                      qb * 256:(qb + 1) * 256],
                            in_=xsq[:, 256:512]),
                            reads=["xsqB"],
                            writes=[f"xscr{do_g}_{qb // 4}"], dma=True)
            if do_f3:
                pf3 = moe_prefetch_list() + moe_prefetch_list()
                emit_stage_dma(pf3.pop(0))
                emit_stage_dma(pf3.pop(0))
            S_.fence()
            if dbg:
                qregs = [f"qscr{h}_{n}" for h in range(H) for n in range(4)]
                S_.op("gpsimd", lambda g: g.dma_start(out=dbgq, in_=qscr),
                      reads=qregs, writes=["dbgq"], dma=True)
                S_.op("gpsimd", lambda g: g.dma_start(out=dbgk, in_=kag),
                      reads=["kag"], writes=["dbgk"], dma=True)
                S_.op("gpsimd", lambda g: g.dma_start(
                      out=dbgv[0:512, :], in_=vag4[0]),
                      reads=["vag4_0"], writes=["dbgv"], dma=True)
                S_.wait_tokens("gpsimd", [S_.regions[r][0][0] for r in ("dbgq", "dbgk", "dbgv")])
            emit_stats_stream(x_scr, xscr_regs[0], 0)
            emit_stats_stream(x_scr, xscr_regs[1], 1)

        y_regs = [[f"y{dc}_{p}" for dc in range(NDC)] for p in range(2)]
        if do_f3:
            pf = pf3 if do_attn else None
            if pf is None:
                pf = moe_prefetch_list() + moe_prefetch_list()
                emit_stage_dma(pf.pop(0))
                emit_stage_dma(pf.pop(0))
            emit_moe_pass(0, x_scr, xscr_regs[0], y_out, y_regs[0], False, pf)
            emit_moe_pass(1, x_scr, xscr_regs[1], y_out, y_regs[1], False, pf)
            assert not pf
        else:
            for p in range(2):
                for dc in range(NDC):
                    S_.op("sync", lambda s, dc=dc, p=p: s.dma_start(
                        out=y_out[dc * 128:(dc + 1) * 128,
                                  p * TP:(p + 1) * TP],
                        in_=x_scr[dc * 128:(dc + 1) * 128,
                                  p * TP:(p + 1) * TP]),
                        reads=[f"xscr{dc}_{p}"],
                        writes=[f"y{dc}_{p}"], dma=True)

        ytoks = []
        for p in range(2):
            for dc in range(NDC):
                reg = S_.regions.get(f"y{dc}_{p}")
                if reg:
                    ytoks.extend(reg[0])
        S_.wait_tokens("sync", ytoks)

        if verbose:
            print(f"sched ops={S_.n_ops()} waits={S_.n_waits()}")
        with nc.Block() as block:
            S_.emit(block)

    return nc


def host_shard(x_full):
    out = []
    xr = np.asarray(x_full).reshape(B, 8, 4, 256, D)
    for r in range(8):
        b, j = r // 4, r % 4
        xc = xr[b, :, j].reshape(TLOC, D)
        out.append(np.ascontiguousarray(xc.T))
    return out


def host_mask(j):
    m = np.zeros((128, 2048), np.float32)
    for qr in range(4):
        for m2 in range(2):
            kk = m2 * 128 + np.arange(128)[:, None]
            qq = np.arange(256)[None, :]
            valid = (256 * qr + kk) <= (256 * j + qq)
            m[:, (qr * 2 + m2) * 256:(qr * 2 + m2 + 1) * 256] = valid
    return m


def kernel(L0_attn_x, L0_attn_q_w, L0_attn_q_b, L0_attn_k_w, L0_attn_k_b,
           L0_attn_v_w, L0_attn_v_b, L0_attn_o_w, L0_attn_o_b,
           L0_ffn_gate, L0_ffn_gate_up_proj, L0_ffn_down_proj):
    nc = build_program()

    xs = host_shard(np.asarray(L0_attn_x, dtype=np.float32))
    biases = np.stack([
        np.asarray(L0_attn_q_b, np.float32).reshape(16, 128).T,
        np.asarray(L0_attn_k_b, np.float32).reshape(16, 128).T,
        np.asarray(L0_attn_v_b, np.float32).reshape(16, 128).T,
        np.asarray(L0_attn_o_b, np.float32).reshape(16, 128).T,
    ], axis=1).reshape(128, 64)
    common = {
        "qw": np.ascontiguousarray(np.asarray(L0_attn_q_w, np.float32).T),
        "kw": np.ascontiguousarray(np.asarray(L0_attn_k_w, np.float32).T),
        "vw": np.ascontiguousarray(np.asarray(L0_attn_v_w, np.float32).T),
        "ow": np.ascontiguousarray(np.asarray(L0_attn_o_w, np.float32).T),
        "biases": np.ascontiguousarray(biases),
        "gate": np.ascontiguousarray(np.asarray(L0_ffn_gate, np.float32)),
        "gup": np.ascontiguousarray(np.asarray(L0_ffn_gate_up_proj, np.float32)),
        "down": np.ascontiguousarray(np.asarray(L0_ffn_down_proj, np.float32)),
    }
    in_maps = []
    for r in range(8):
        m = dict(common)
        m["x"] = xs[r]
        m["maskq"] = host_mask(r % 4)
        in_maps.append(m)

    res = run_bass_kernel_spmd(nc, in_maps, core_ids=list(range(8)))

    out = np.zeros((B, 8, 4, 256, D), np.float32)
    for r in range(8):
        b, j = r // 4, r % 4
        yc = np.asarray(res.results[r]["y"])
        out[b, :, j] = yc.T.reshape(8, 256, D)
    return out.reshape(B, SEQ, D)


# version 13
# speedup vs baseline: 1.0117x; 1.0049x over previous
import numpy as np
import concourse.bass as bass
import concourse.mybir as mybir
from concourse.bass_utils import run_bass_kernel_spmd

F32 = mybir.dt.float32
BF16 = mybir.dt.bfloat16
AF = mybir.ActivationFunctionType
ALU = mybir.AluOpType
AX = mybir.AxisListType

B, SEQ, D = 2, 8192, 2048
H, HD = 16, 128
SCALE = 1.0 / float(np.sqrt(HD))
NE = 32
FH = 768
NI = 768
TLOC = 2048
TP = 1024
NDC = D // 128
NFC = FH // 128
NIC = NI // 128
EPS = 1e-5
BIG = 1e30

ENGINES = ("sync", "tensor", "vector", "scalar", "gpsimd")
NDMASEM = 6


class Sched:

    def __init__(self, nc, es):
        self.nc = nc
        self.esem = {}
        self.key2sem = {}
        for e in ENGINES:
            s = es.enter_context(nc.semaphore(f"prog_{e}"))
            self.esem[e] = ("E" + e, s)
            self.key2sem["E" + e] = s
        self.dsem = {}
        for e in ENGINES:
            lst = []
            for i in range(NDMASEM):
                s = es.enter_context(nc.semaphore(f"dma_{e}_{i}"))
                key = f"D{e}{i}"
                lst.append((key, s))
                self.key2sem[key] = s
            self.dsem[e] = lst
        self.cnt = {}
        self.dmause = {e: 0 for e in ENGINES}
        self.streams = {e: [] for e in ENGINES}
        self.clock = {e: {} for e in ENGINES}
        self.snap = {}
        self.regions = {}

    def add_sem(self, name, es):
        s = es.enter_context(self.nc.semaphore(name))
        key = "X" + name
        self.key2sem[key] = s
        return (key, s)

    def _edge(self, need, token):
        k, v = token
        if need.get(k, 0) < v:
            need[k] = v

    def op(self, eng, fn, reads=(), writes=(), dma=False, extra_tokens=(),
           sem_override=None):
        need = {}
        for r in reads:
            reg = self.regions.get(r)
            if reg:
                for t in reg[0]:
                    self._edge(need, t)
        for w in writes:
            reg = self.regions.get(w)
            if reg:
                for t in reg[0]:
                    self._edge(need, t)
                for rk, rv in reg[1].items():
                    self._edge(need, (rk, rv))
        for t in extra_tokens:
            self._edge(need, t)
        if sem_override is not None:
            key, sem = sem_override
            delta = 1
            myval = self.cnt.get(key, 0) + 1
        elif dma:
            i = self.dmause[eng] % NDMASEM
            self.dmause[eng] += 1
            key, sem = self.dsem[eng][i]
            prev = self.cnt.get(key, 0)
            if prev:
                self._edge(need, (key, prev))
            delta = 16
            myval = prev + 16
        else:
            key, sem = self.esem[eng]
            delta = 1
            myval = self.cnt.get(key, 0) + 1
        clk = self.clock[eng]
        waits = []
        for k, v in sorted(need.items()):
            if clk.get(k, 0) < v:
                waits.append((self.key2sem[k], v))
                psnap = self.snap.get((k, v))
                if psnap:
                    for kk, vv in psnap.items():
                        if clk.get(kk, 0) < vv:
                            clk[kk] = vv
                if clk.get(k, 0) < v:
                    clk[k] = v
        self.cnt[key] = myval
        self.streams[eng].append((fn, waits, key, sem, delta, myval))
        snapshot = dict(clk)
        snapshot[key] = myval
        self.snap[(key, myval)] = snapshot
        token = (key, myval)
        for r in reads:
            reg = self.regions.setdefault(r, [[], {}])
            reg[1][key] = max(reg[1].get(key, 0), myval)
        for w in writes:
            self.regions[w] = [[token], {}]
        return token

    def set_write(self, region, tokens):
        self.regions[region] = [list(tokens), {}]

    def wait_tokens(self, eng, tokens):
        need = {}
        for t in tokens:
            self._edge(need, t)
        clk = self.clock[eng]
        waits = []
        for k, v in sorted(need.items()):
            if clk.get(k, 0) < v:
                waits.append((self.key2sem[k], v))
                psnap = self.snap.get((k, v))
                if psnap:
                    for kk, vv in psnap.items():
                        if clk.get(kk, 0) < vv:
                            clk[kk] = vv
                clk[k] = max(clk.get(k, 0), v)
        self.streams[eng].append((None, waits, None, None, 0, 0))

    def fence(self):
        toks = [(k, v) for k, v in self.cnt.items()]
        for e in ENGINES:
            self.wait_tokens(e, toks)

    def n_ops(self):
        return {e: len(s) for e, s in self.streams.items()}

    def n_waits(self):
        return sum(len(w) for s in self.streams.values() for st in s
                   for w in [st[1]])

    def emit(self, block):
        waited = set()
        for s in self.streams.values():
            for st in s:
                for sem_v in st[1]:
                    pass
        waited_vals = {}
        for s in self.streams.values():
            for st in s:
                for semobj, v in st[1]:
                    waited_vals.setdefault(id(semobj), set()).add(v)

        def mk(name):
            ekey, esem = self.esem[name]
            wset = waited_vals.get(id(esem), set())

            def body(engine):
                pending = 0
                for fn, waits, key, sem, delta, myval in self.streams[name]:
                    for s, v in waits:
                        engine.wait_ge(s, v)
                    if fn is None:
                        continue
                    instr = fn(engine)
                    if key == ekey:
                        pending += delta
                        if myval in wset:
                            instr.then_inc(sem, pending)
                            pending = 0
                    else:
                        instr.then_inc(sem, delta)
            return body
        block.sync(mk("sync"))
        block.tensor(mk("tensor"))
        block.vector(mk("vector"))
        block.scalar(mk("scalar"))
        block.gpsimd(mk("gpsimd"))


def build_program(n_experts=NE, do_attn=True, do_f3=True, verbose=False, dbg=False):
    nc = bass.Bass("TRN2", target_bir_lowering=False, debug=False)

    x_in = nc.dram_tensor("x", [D, TLOC], F32, kind="ExternalInput").ap()
    qw = nc.dram_tensor("qw", [D, D], F32, kind="ExternalInput").ap()
    kw = nc.dram_tensor("kw", [D, D], F32, kind="ExternalInput").ap()
    vw = nc.dram_tensor("vw", [D, D], F32, kind="ExternalInput").ap()
    ow = nc.dram_tensor("ow", [D, D], F32, kind="ExternalInput").ap()
    biases = nc.dram_tensor("biases", [128, 64], F32, kind="ExternalInput").ap()
    gate = nc.dram_tensor("gate", [D, NE], F32, kind="ExternalInput").ap()
    gup = nc.dram_tensor("gup", [NE, D, 2 * FH], F32, kind="ExternalInput").ap()
    down = nc.dram_tensor("down", [NE, NI, D], F32, kind="ExternalInput").ap()
    maskq = nc.dram_tensor("maskq", [128, 2048], F32, kind="ExternalInput").ap()
    y_out = nc.dram_tensor("y", [D, TLOC], F32, kind="ExternalOutput").ap()

    dbgq = dbgk = dbgv = None
    if dbg:
        dbgq = nc.dram_tensor("dbgq", [D, TLOC], F32, kind="ExternalOutput").ap()
        dbgx = nc.dram_tensor("dbgx", [D, TLOC], F32, kind="ExternalOutput").ap()
        dbgk = nc.dram_tensor("dbgk", [4 * D, TLOC], F32, kind="ExternalOutput").ap()
        dbgv = nc.dram_tensor("dbgv", [4 * TLOC, D // 2], F32, kind="ExternalOutput").ap()
    x_scr = nc.dram_tensor("x_scr", [D, TLOC], F32, kind="Internal").ap()
    qscr = nc.dram_tensor("qscr", [D, TLOC], BF16, kind="Internal").ap()
    kag_in = nc.dram_tensor("kag_in", [D, TLOC], BF16, kind="Internal").ap()
    vag_in4 = [nc.dram_tensor(f"vag_in{g}", [128, 4 * 2048], BF16,
                              kind="Internal").ap() for g in range(4)]
    kag = nc.dram_tensor("kag", [4 * D, TLOC], BF16, kind="Internal").ap()
    vag4 = [nc.dram_tensor(f"vag{g}", [4 * 128, 4 * 2048], BF16,
                           kind="Internal").ap() for g in range(4)]

    from contextlib import ExitStack
    es = ExitStack()
    sb = lambda n, s, d: es.enter_context(nc.sbuf_tensor(n, s, d))
    psa = lambda n, s, d: es.enter_context(nc.psum_tensor(n, s, d))
    with es:
        wslot = [sb(f"wslot{i}", [128, 12288], BF16) for i in range(2)]
        xch = [sb(f"xch{i}", [128, TP], F32) for i in range(2)]
        xsq = sb("xsq", [128, TP], F32)
        wfms = sb("wfms", [NE, TP], BF16)
        wrow = sb("wrow", [1, TP], BF16)
        w_b = sb("w_b", [128, TP], BF16)
        gateb = sb("gateb", [128, NDC * NE], BF16)
        qkvob = sb("qkvob", [128, 80], F32)
        stA = sb("stA", [65, TLOC], F32)
        srow2 = sb("srow2", [1, 512], F32)
        srowB = sb("srowB", [1, 512], F32)
        onest = sb("onest", [65, 128], F32)
        rsc = sb("rsc", [NE, TP], F32)
        ltm = sb("ltm", [128, 8 * NE], F32)
        wtm = sb("wtm", [128, 8 * NE], F32)
        lcur = sb("lcur", [128, NE], F32)
        eqb = sb("eqb", [128, NE], F32)
        sel32 = sb("sel32", [128, NE], F32)
        m1b = sb("m1b", [128, 2], F32)
        ident = sb("ident", [128, 128], BF16)
        identf = sb("identf", [128, 128], F32)
        onesr = sb("onesr", [1, 128], F32)
        onesrb = sb("onesrb", [1, 128], BF16)
        onesc = sb("onesc", [128, 1], F32)
        onescb = sb("onescb", [128, 1], BF16)

        pgt = [psa(f"pg{i}", [128, 512], F32) for i in range(4)]
        pdt = [psa(f"pd{i}", [128, 512], F32) for i in range(2)]
        pbt = psa("pb", [128, 512], F32)
        pxt = psa("px", [128, 512], F32)
        pgu = [t[:, :] for t in pgt]
        NGU = 4
        pdn = [t[:, :] for t in pdt]
        pbc = pbt[:, :]
        pex2 = pxt[:, :]
        ptr = pxt[:, :].bitcast(BF16)
        sspairs = [(pgu[0], pgu[1]), (pgu[2], pgu[3])]
        pstRegs = [("pgu0", "pgu1"), ("pgu2", "pgu3")]
        pso = pdn[0]
        psl = pdt[1][0:1, 0:256]
        plinv = pbt[:, 0:256]

        S_ = Sched(nc, es)
        cck = S_.add_sem("cck", es)
        ccv = S_.add_sem("ccv", es)
        ccv2 = S_.add_sem("ccv2", es)

        def cast_load(dst_sb, src_ap, nchunk, width, region, per=4):
            toks = []
            for a in range(0, nchunk, per):
                k = min(per, nchunk - a)
                src = src_ap[a * 128:(a + k) * 128, :].rearrange(
                    "(dc p) f -> p dc f", p=128)
                dst = dst_sb[:, a * width:(a + k) * width].rearrange(
                    "p (dc f) -> p dc f", f=width)
                toks.append(S_.op("gpsimd", lambda g, src=src, dst=dst:
                                  g.dma_start(out=dst, in_=src), dma=True,
                                  writes=[region] if a == 0 else ()))
            S_.set_write(region, toks)
            return toks

        S_.op("vector", lambda v: v.memset(xsq[:, 0:128], 1.0), writes=["xsq"])
        S_.op("gpsimd", lambda g: g.affine_select(
            out=identf[:, :], in_=xsq[:, 0:128], pattern=[[1, 128]],
            compare_op=ALU.is_equal, fill=0.0, base=0, channel_multiplier=-1),
            reads=["xsq"], writes=["identf"])
        S_.op("vector", lambda v: v.tensor_copy(out=ident[:, :], in_=identf[:, :]),
              reads=["identf"], writes=["ident"])
        S_.op("vector", lambda v: v.memset(onesr[:, :], 1.0), writes=["onesr"])
        S_.op("vector", lambda v: v.memset(onesrb[:, :], 1.0), writes=["onesrb"])
        S_.op("vector", lambda v: v.memset(onest[0:1, :], 1.0 / D), writes=["onest0"])
        S_.op("vector", lambda v: v.memset(onest[64:65, :], 1.0), writes=["onest1"])
        S_.op("vector", lambda v: v.memset(onesc[:, :], 1.0), writes=["onesc"])
        S_.op("vector", lambda v: v.memset(onescb[:, :], 1.0), writes=["onescb"])
        cast_load(gateb, gate, NDC, NE, "gateb", per=16)
        S_.op("sync", lambda s: s.dma_start(out=qkvob[:, 0:64], in_=biases),
              writes=["qkvob"], dma=True)
        S_.op("vector", lambda v: v.tensor_scalar(
            out=qkvob[:, 64:80], in0=qkvob[:, 0:16], scalar1=float(SCALE),
            scalar2=None, op0=ALU.mult), reads=["qkvob"], writes=["qkvob2"])

        def emit_stats_finalize(p):
            for n in range(2):
                sl = slice(p * TP + n * 512, p * TP + (n + 1) * 512)
                S_.op("vector", lambda v, n=n, sl=sl: v.tensor_copy(
                    out=stA[0:1, sl], in_=pgu[n][0:1, 0:512]),
                    reads=[f"pgu{n}"], writes=[f"stA0_{p}_{n}"])
                S_.op("vector", lambda v, n=n, sl=sl: v.tensor_tensor(
                    out=srow2[:, :], in0=stA[0:1, sl], in1=stA[0:1, sl],
                    op=ALU.mult), reads=[f"stA0_{p}_{n}"], writes=["srow2"])
                S_.op("vector", lambda v, n=n: v.scalar_tensor_tensor(
                    out=srowB[:, :], in0=srow2[:, :], scalar=-1.0 / D,
                    in1=pdn[n][0:1, 0:512], op0=ALU.mult, op1=ALU.add),
                    reads=["srow2", f"pdn{n}"], writes=["srowB"])
                S_.op("vector", lambda v: v.tensor_scalar(
                    out=srow2[:, :], in0=srowB[:, :], scalar1=1.0 / D,
                    scalar2=float(EPS), op0=ALU.mult, op1=ALU.add),
                    reads=["srowB"], writes=["srow2"])
                S_.op("scalar", lambda s: s.activation(
                    out=srowB[:, :], in_=srow2[:, :], func=AF.Sqrt),
                    reads=["srow2"], writes=["srowB"])
                S_.op("vector", lambda v, sl=sl: v.reciprocal(
                    out=stA[64:65, sl], in_=srowB[:, :]),
                    reads=["srowB"], writes=[f"stA1_{p}_{n}"])

        def emit_stats_stream(src_ap, src_regs, p):
            sx_tok = [None, None]
            sq_tok = [None, None]
            for dc in range(NDC):
                buf = xch[dc % 2]
                breg = f"xch{dc % 2}"
                S_.op("sync", lambda s, dc=dc, buf=buf: s.dma_start(
                    out=buf[:, :], in_=src_ap[dc * 128:(dc + 1) * 128,
                                              p * TP:(p + 1) * TP]),
                    reads=[src_regs[dc]] if src_regs else (),
                    writes=[breg], dma=True)
                S_.op("scalar", lambda s, buf=buf: s.activation(
                    out=xsq[:, :], in_=buf[:, :], func=AF.Square),
                    reads=[breg], writes=["xsq"])
                for n in range(2):
                    sx_tok[n] = S_.op(
                        "tensor", lambda t_, buf=buf, n=n, dc=dc: t_.matmul(
                            pgu[n][0:1, 0:512], onesc[:, :],
                            buf[:, n * 512:(n + 1) * 512],
                            start=(dc == 0), stop=(dc == NDC - 1)),
                        reads=["onesc", breg],
                        writes=[f"pgu{n}"] if dc == 0 else [])
                    sq_tok[n] = S_.op(
                        "tensor", lambda t_, n=n, dc=dc: t_.matmul(
                            pdn[n][0:1, 0:512], onesc[:, :],
                            xsq[:, n * 512:(n + 1) * 512],
                            start=(dc == 0), stop=(dc == NDC - 1)),
                        reads=["onesc", "xsq"],
                        writes=[f"pdn{n}"] if dc == 0 else [])
            for n in range(2):
                S_.set_write(f"pgu{n}", [sx_tok[n]])
                S_.set_write(f"pdn{n}", [sq_tok[n]])
            emit_stats_finalize(p)

        def emit_ln_normalize(src_ap, src_regs, toff, xnb_t, xnb_regs,
                              colw, coff):
            p = toff // TP
            for n in range(2):
                sl = slice(toff + n * 512, toff + (n + 1) * 512)
                S_.op("tensor", lambda t, n=n, sl=sl: t.matmul(
                    pdn[n], onest[0:1, :],
                    stA[0:1, sl], start=True, stop=True),
                    reads=["onest0", f"stA0_{p}_{n}"], writes=[f"pdn{n}"])
                S_.op("tensor", lambda t, n=n, sl=sl: t.matmul(
                    pbc if n == 0 else pex2, onest[64:65, :],
                    stA[64:65, sl], start=True, stop=True),
                    reads=["onest1", f"stA1_{p}_{n}"],
                    writes=["pbc" if n == 0 else "pex2"])
            for dc in range(NDC):
                buf = xch[dc % 2]
                breg = f"xch{dc % 2}"
                S_.op("sync", lambda s, dc=dc, buf=buf: s.dma_start(
                    out=buf[:, :], in_=src_ap[dc * 128:(dc + 1) * 128,
                                              toff:toff + TP]),
                    reads=[src_regs[dc]] if src_regs else (),
                    writes=[breg], dma=True)
                for n in range(2):
                    S_.op("vector", lambda v, buf=buf, n=n: v.tensor_tensor(
                        out=xsq[:, n * 512:(n + 1) * 512],
                        in0=buf[:, n * 512:(n + 1) * 512], in1=pdn[n],
                        op=ALU.subtract),
                        reads=[breg, f"pdn{n}"], writes=["xsq"])
                    S_.op("vector",
                          lambda v, dc=dc, n=n, xnb_t=xnb_t: v.tensor_tensor(
                              out=xnb_t[:, dc * colw + coff + n * 512:
                                        dc * colw + coff + (n + 1) * 512],
                              in0=xsq[:, n * 512:(n + 1) * 512],
                              in1=pbc if n == 0 else pex2, op=ALU.mult),
                          reads=["xsq", "pbc" if n == 0 else "pex2"],
                          writes=[xnb_regs[dc]])

        def emit_router(xnb_t, xnb_regs, colw, coff):
            for n in range(2):
                dstp = pbc if n == 0 else pex2
                dreg = "pbc" if n == 0 else "pex2"
                tok = None
                for dc in range(NDC):
                    tok = S_.op(
                        "tensor", lambda t, dc=dc, n=n, dstp=dstp: t.matmul(
                            dstp[0:NE, 0:512],
                            gateb[:, dc * NE:(dc + 1) * NE],
                            xnb_t[:, dc * colw + coff + n * 512:
                                  dc * colw + coff + (n + 1) * 512],
                            start=(dc == 0), stop=(dc == NDC - 1)),
                        reads=["gateb", xnb_regs[dc]],
                        writes=[dreg] if dc == 0 else [])
                S_.set_write(dreg, [tok])
                S_.op("vector", lambda v, n=n, dstp=dstp: v.tensor_copy(
                    out=rsc[:, n * 512:(n + 1) * 512], in_=dstp[0:NE, 0:512]),
                    reads=[dreg], writes=["rsc"])
            for tt in range(8):
                dstp = pbc if tt % 2 == 0 else pex2
                dreg = "pbc" if tt % 2 == 0 else "pex2"
                S_.op("tensor", lambda t, tt=tt, dstp=dstp: t.transpose(
                    dstp[:, 0:NE], rsc[0:NE, tt * 128:(tt + 1) * 128],
                    identf[0:NE, 0:NE]),
                    reads=["rsc", "identf"], writes=[dreg])
                S_.op("vector", lambda v, tt=tt, dstp=dstp: v.tensor_copy(
                    out=ltm[:, tt * NE:(tt + 1) * NE], in_=dstp[:, 0:NE]),
                    reads=[dreg], writes=[f"ltm{tt}"])
            for tt in range(8):
                lslice = ltm[:, tt * NE:(tt + 1) * NE]
                S_.op("vector", lambda v, lslice=lslice: v.tensor_copy(
                    out=lcur[:, :], in_=lslice),
                    reads=[f"ltm{tt}"], writes=["lcur"])
                for it in range(4):
                    S_.op("vector", lambda v: v.reduce_max(
                        m1b[:, 0:1], lcur[:, :], axis=AX.X),
                        reads=["lcur"], writes=["m1max"])
                    if it == 0:
                        S_.op("vector", lambda v: v.tensor_scalar(
                            out=m1b[:, 1:2], in0=m1b[:, 0:1], scalar1=-1.0,
                            scalar2=None, op0=ALU.mult),
                            reads=["m1max"], writes=["m1neg"])
                    S_.op("vector", lambda v: v.tensor_scalar(
                        out=eqb[:, :], in0=lcur[:, :], scalar1=m1b[:, 0:1],
                        scalar2=None, op0=ALU.is_equal),
                        reads=["lcur", "m1max"], writes=["eqb"])
                    if it == 0:
                        S_.op("vector", lambda v: v.tensor_copy(
                            out=sel32[:, :], in_=eqb[:, :]),
                            reads=["eqb"], writes=["sel32"])
                    else:
                        S_.op("vector", lambda v: v.tensor_tensor(
                            out=sel32[:, :], in0=sel32[:, :], in1=eqb[:, :],
                            op=ALU.add), reads=["sel32", "eqb"],
                            writes=["sel32"])
                    if it < 3:
                        S_.op("vector", lambda v: v.scalar_tensor_tensor(
                            out=lcur[:, :], in0=eqb[:, :], scalar=-BIG,
                            in1=lcur[:, :], op0=ALU.mult, op1=ALU.add),
                            reads=["eqb", "lcur"], writes=["lcur"])
                S_.op("scalar", lambda s, lslice=lslice: s.activation(
                    out=eqb[:, :], in_=lslice, func=AF.Exp, bias=m1b[:, 1:2]),
                    reads=[f"ltm{tt}", "m1neg"], writes=["eqb"])
                S_.op("vector", lambda v: v.tensor_tensor(
                    out=eqb[:, :], in0=eqb[:, :], in1=sel32[:, :],
                    op=ALU.mult), reads=["eqb", "sel32"], writes=["eqb"])
                S_.op("vector", lambda v: v.reduce_sum(
                    m1b[:, 0:1], eqb[:, :], axis=AX.X),
                    reads=["eqb"], writes=["m1max"])
                S_.op("vector", lambda v: v.reciprocal(
                    out=m1b[:, 1:2], in_=m1b[:, 0:1]),
                    reads=["m1max"], writes=["m1neg"])
                S_.op("vector", lambda v, tt=tt: v.tensor_scalar(
                    out=wtm[:, tt * NE:(tt + 1) * NE], in0=eqb[:, :],
                    scalar1=m1b[:, 1:2], scalar2=0.5, op0=ALU.mult,
                    op1=ALU.mult), reads=["eqb", "m1neg"],
                    writes=[f"wtm{tt}"])
            wtoks = []
            for tt in range(8):
                dstp = pbc if tt % 2 == 0 else pex2
                dreg = "pbc" if tt % 2 == 0 else "pex2"
                S_.op("tensor", lambda t, tt=tt, dstp=dstp: t.transpose(
                    dstp[0:NE, 0:128], wtm[:, tt * NE:(tt + 1) * NE],
                    identf[:, :]),
                    reads=[f"wtm{tt}", "identf"], writes=[dreg])
                wtoks.append(S_.op("vector", lambda v, tt=tt, dstp=dstp:
                                   v.tensor_copy(
                                       out=wfms[:, tt * 128:(tt + 1) * 128],
                                       in_=dstp[0:NE, 0:128]),
                                   reads=[dreg],
                                   writes=["wfms"] if tt == 0 else []))
            S_.set_write("wfms", wtoks)

        stage_counter = [0]
        consume_counter = [0]

        def emit_stage_dma(desc):
            sid = stage_counter[0]
            stage_counter[0] += 1
            slot = wslot[sid % 2]
            reg = f"wslot{sid % 2}"
            kind = desc[0]
            if kind == "g":
                cast_load(slot, gup[desc[1], :, 0:FH], NDC, FH, reg)
            elif kind == "u":
                cast_load(slot, gup[desc[1], :, FH:2 * FH], NDC, FH, reg)
            elif kind == "dn":
                cast_load(slot, down[desc[1]], NIC, D, reg, per=3)
            elif kind == "proj":
                mat, qtr = desc[1], desc[2]
                cast_load(slot, mat[:, qtr * 512:(qtr + 1) * 512],
                          NDC, 512, reg)
            return sid

        def emit_moe_pass(p, src_ap, src_regs, dst_ap, dst_regs, with_stats,
                          prefetch):
            emit_ln_normalize(src_ap, src_regs, p * TP, xnb, xnb_regs, TP, 0)
            emit_router(xnb, xnb_regs, TP, 0)
            for e in range(n_experts):
                S_.op("sync", lambda s, e=e: s.dma_start(
                    out=wrow[:, :], in_=wfms[e:e + 1, :]),
                    reads=["wfms"], writes=["wrow"], dma=True)
                for n in range(2):
                    dstp = pbc if n == 0 else pex2
                    dreg = "pbc" if n == 0 else "pex2"
                    S_.op("tensor", lambda t, n=n, dstp=dstp: t.matmul(
                        dstp, onesrb[:, :],
                        wrow[0:1, n * 512:(n + 1) * 512],
                        start=True, stop=True),
                        reads=["onesrb", "wrow"], writes=[dreg])
                    S_.op("vector", lambda v, n=n, dstp=dstp: v.tensor_copy(
                        out=w_b[:, n * 512:(n + 1) * 512], in_=dstp),
                        reads=[dreg], writes=[f"w_b{n}"])
                for kind in ("g", "u", "dn"):
                    sid = consume_counter[0]
                    consume_counter[0] += 1
                    slot = wslot[sid % 2]
                    sreg = f"wslot{sid % 2}"
                    if kind in ("g", "u"):
                        for f in range(NFC):
                            for n in range(2):
                                par = (f * 2 + n) % NGU
                                preg = f"pgu{par}"
                                tok = None
                                for dc in range(NDC):
                                    tok = S_.op(
                                        "tensor",
                                        lambda t, f=f, n=n, dc=dc, par=par,
                                        slot=slot: t.matmul(
                                            pgu[par],
                                            slot[:, dc * FH + f * 128:
                                                 dc * FH + (f + 1) * 128],
                                            xnb[:, dc * TP + n * 512:
                                                dc * TP + (n + 1) * 512],
                                            start=(dc == 0),
                                            stop=(dc == NDC - 1)),
                                        reads=[sreg, xnb_regs[dc]],
                                        writes=[preg] if dc == 0 else [])
                                S_.set_write(preg, [tok])
                                if kind == "g":
                                    S_.op("scalar",
                                          lambda s, f=f, n=n, par=par:
                                          s.activation(
                                              out=silu_g[:,
                                                         f * TP + n * 512:
                                                         f * TP + (n + 1) * 512],
                                              in_=pgu[par],
                                              func=AF.Silu),
                                          reads=[preg],
                                          writes=[f"silu{f}_{n}"])
                                else:
                                    uwv = xsq[:, :].bitcast(BF16)
                                    S_.op("vector",
                                          lambda v, n=n, par=par, uwv=uwv:
                                          v.tensor_tensor(
                                              out=uwv[:, n * 512:(n + 1) * 512],
                                              in0=pgu[par],
                                              in1=w_b[:, n * 512:(n + 1) * 512],
                                              op=ALU.mult),
                                          reads=[preg, f"w_b{n}"],
                                          writes=["xsq"])
                                    S_.op("vector",
                                          lambda v, f=f, n=n, uwv=uwv:
                                          v.tensor_tensor(
                                              out=gated[:, f * TP + n * 512:
                                                        f * TP + (n + 1) * 512],
                                              in0=uwv[:, n * 512:(n + 1) * 512],
                                              in1=silu_g[:, f * TP + n * 512:
                                                         f * TP + (n + 1) * 512],
                                              op=ALU.mult),
                                          reads=["xsq", f"silu{f}_{n}"],
                                          writes=[f"gated{f}_{n}"])
                    else:
                        for do in range(NDC):
                            for n in range(2):
                                par = (do * 2 + n) % 2
                                preg = f"pdn{par}"
                                tok = None
                                for ic in range(NIC):
                                    tok = S_.op(
                                        "tensor",
                                        lambda t, do=do, n=n, ic=ic, par=par,
                                        slot=slot: t.matmul(
                                            pdn[par],
                                            slot[:, ic * D + do * 128:
                                                 ic * D + (do + 1) * 128],
                                            gated[:, ic * TP + n * 512:
                                                  ic * TP + (n + 1) * 512],
                                            start=(ic == 0),
                                            stop=(ic == NIC - 1)),
                                        reads=[sreg, f"gated{ic}_{n}"],
                                        writes=[preg] if ic == 0 else [])
                                S_.set_write(preg, [tok])
                                dst = acc[:, do * TP + n * 512:
                                          do * TP + (n + 1) * 512]
                                if e == 0:
                                    S_.op("vector",
                                          lambda v, dst=dst, par=par:
                                          v.tensor_copy(out=dst,
                                                        in_=pdn[par]),
                                          reads=[preg],
                                          writes=[f"acc{do}_{n}"])
                                else:
                                    S_.op("vector",
                                          lambda v, dst=dst, par=par:
                                          v.tensor_tensor(
                                              out=dst, in0=pdn[par],
                                              in1=dst, op=ALU.add),
                                          reads=[preg, f"acc{do}_{n}"],
                                          writes=[f"acc{do}_{n}"])
                    if prefetch:
                        emit_stage_dma(prefetch.pop(0))
            sx_tok = [None, None]
            sq_tok = [None, None]
            for dc in range(NDC):
                buf = xch[dc % 2]
                breg = f"xch{dc % 2}"
                S_.op("sync", lambda s, dc=dc, buf=buf: s.dma_start(
                    out=buf[:, :], in_=src_ap[dc * 128:(dc + 1) * 128,
                                              p * TP:(p + 1) * TP]),
                    reads=[src_regs[dc]] if src_regs else (),
                    writes=[breg], dma=True)
                S_.op("vector", lambda v, dc=dc, buf=buf: v.tensor_tensor(
                    out=xsq[:, :], in0=buf[:, :],
                    in1=acc[:, dc * TP:(dc + 1) * TP], op=ALU.add),
                    reads=[breg] + [f"acc{dc}_{n}" for n in range(2)],
                    writes=["xsq"])
                S_.op("sync", lambda s, dc=dc: s.dma_start(
                    out=dst_ap[dc * 128:(dc + 1) * 128, p * TP:(p + 1) * TP],
                    in_=xsq[:, :]),
                    reads=["xsq"], writes=[dst_regs[dc]], dma=True)
                if with_stats:
                    S_.op("scalar", lambda s, buf=buf: s.activation(
                        out=buf[:, :], in_=xsq[:, :], func=AF.Square),
                        reads=["xsq"], writes=[breg])
                    for n in range(2):
                        sx_tok[n] = S_.op(
                            "tensor", lambda t_, n=n, dc=dc: t_.matmul(
                                pgu[n][0:1, 0:512], onesc[:, :],
                                xsq[:, n * 512:(n + 1) * 512],
                                start=(dc == 0), stop=(dc == NDC - 1)),
                            reads=["onesc", "xsq"],
                            writes=[f"pgu{n}"] if dc == 0 else [])
                        sq_tok[n] = S_.op(
                            "tensor", lambda t_, buf=buf, n=n, dc=dc: t_.matmul(
                                pdn[n][0:1, 0:512], onesc[:, :],
                                buf[:, n * 512:(n + 1) * 512],
                                start=(dc == 0), stop=(dc == NDC - 1)),
                            reads=["onesc", breg],
                            writes=[f"pdn{n}"] if dc == 0 else [])
            if with_stats:
                for n in range(2):
                    S_.set_write(f"pgu{n}", [sx_tok[n]])
                    S_.set_write(f"pdn{n}", [sq_tok[n]])
                emit_stats_finalize(p)

        xscr_regs = [[f"xscr{dc}_{p}" for dc in range(NDC)] for p in range(2)]

        acc = sb("acc", [128, NDC * TP], F32)
        silu_g = sb("silu_g", [128, NFC * TP], BF16)
        gated = sb("gated", [128, NFC * TP], BF16)
        xnb = sb("xnb", [128, NDC * TP], BF16)
        xnb_regs = [f"xnb{dc}" for dc in range(NDC)]

        emit_stats_stream(x_in, None, 0)
        emit_stats_stream(x_in, None, 1)

        def moe_prefetch_list():
            return [(k, e) for e in range(n_experts) for k in ("g", "u", "dn")]

        pf = moe_prefetch_list() + moe_prefetch_list()
        emit_stage_dma(pf.pop(0))
        emit_stage_dma(pf.pop(0))
        emit_moe_pass(0, x_in, None, x_scr, xscr_regs[0], True, pf)
        emit_moe_pass(1, x_in, None, x_scr, xscr_regs[1], True, pf)
        assert not pf

        if do_attn:
            S_.fence()
            xnbA_ap = acc[:, :].bitcast(BF16)
            xnbA_regs = [f"xnbA{dc}" for dc in range(NDC)]
            for hp in range(2):
                emit_ln_normalize(x_scr, xscr_regs[hp], hp * TP,
                                  xnbA_ap, xnbA_regs, 2048, hp * TP)
            if dbg:
                xtoks = []
                for dc in range(NDC):
                    xtoks.append(S_.op("gpsimd",
                        lambda g, dc=dc: g.dma_start(
                            out=dbgx[dc * 128:(dc + 1) * 128, :],
                            in_=xnbA_ap[:, dc * 2048:(dc + 1) * 2048]),
                        reads=[xnbA_regs[dc]], writes=["dbgx"] if dc == 0 else (),
                        dma=True))
                S_.set_write("dbgx", xtoks)
                S_.op("vector", lambda v: v.engine_nop(), reads=["dbgx"])
            qstag = gated[:, 0:512]
            kstag = gated[:, 512:1024]
            vstag = gated[:, 1024:1536]
            vstg2 = gated[:, 1536:2048]
            dumped = [False]
            for mi, (mat, dst_dram, dname, bcol, scl) in (
                    (1, (kw, kag_in, "kagin", 16, None)),
                    (2, (vw, vag_in4, "vagin", 32, None)),
                    (0, (qw, qscr, "qscr", 64, None))):
                for qtr in range(4):
                    emit_stage_dma(("proj", mat, qtr))
                    sid = stage_counter[0] - 1
                    consume_counter[0] = stage_counter[0]
                    slot = wslot[sid % 2]
                    sreg = f"wslot{sid % 2}"
                    if dbg and not dumped[0]:
                        dumped[0] = True
                        for a4 in range(4):
                            S_.op("gpsimd", lambda g, a4=a4, slot=slot:
                                  g.dma_start(
                                      out=dbgx[a4 * 128:(a4 + 1) * 128, :],
                                      in_=slot[:, a4 * 2048:(a4 + 1) * 2048]),
                                  reads=[sreg],
                                  writes=["dbgslot"] if a4 == 0 else (),
                                  dma=True)
                        S_.op("vector", lambda v: v.engine_nop(),
                              reads=["dbgslot"])
                    for hh in range(4):
                        h = qtr * 4 + hh
                        for n in range(4):
                            par = (hh * 4 + n) % NGU
                            preg = f"pgu{par}"
                            tok = None
                            for dc in range(NDC):
                                tok = S_.op(
                                    "tensor",
                                    lambda t, hh=hh, n=n, dc=dc, par=par,
                                    slot=slot: t.matmul(
                                        pgu[par],
                                        slot[:, dc * 512 + hh * 128:
                                             dc * 512 + (hh + 1) * 128],
                                        xnbA_ap[:, dc * 2048 + n * 512:
                                                dc * 2048 + (n + 1) * 512],
                                        start=(dc == 0), stop=(dc == NDC - 1)),
                                    reads=[sreg, xnbA_regs[dc]],
                                    writes=[preg] if dc == 0 else [])
                            S_.set_write(preg, [tok])
                            if mi == 0:
                                if dbg and h == 0 and n == 0:
                                    S_.op("gpsimd", lambda g, par=par:
                                          g.dma_start(
                                              out=dbgx[512:640, 0:512].bitcast(F32) if False else dbgx[512:640, 0:512],
                                              in_=pgu[par])
                                          if False else g.dma_start(
                                              out=dbgx[640:768, 0:512],
                                              in_=xnbA_ap[:, 0:512]),
                                          reads=[preg, xnbA_regs[0]],
                                          writes=["dbgrhs"], dma=True)
                                S_.op("scalar",
                                      lambda s, par=par, h=h: s.activation(
                                          out=qstag, in_=pgu[par],
                                          func=AF.Identity, scale=float(SCALE),
                                          bias=qkvob[:, 64 + h:65 + h]),
                                      reads=[preg, "qkvob2"],
                                      writes=["qstag"])
                                if dbg and h == 0 and n == 0:
                                    S_.op("gpsimd", lambda g: g.dma_start(
                                          out=dbgx[512:640, 0:512],
                                          in_=qstag),
                                          reads=["qstag"],
                                          writes=["dbgqs"], dma=True)
                                    S_.op("vector", lambda v: v.engine_nop(),
                                          reads=["dbgqs", "dbgrhs"])
                                S_.op("sync", lambda s, h=h, n=n: s.dma_start(
                                    out=qscr[h * 128:(h + 1) * 128,
                                             n * 512:(n + 1) * 512],
                                    in_=qstag),
                                    reads=["qstag"],
                                    writes=[f"qscr{h}_{n}"], dma=True)
                            elif mi == 1:
                                S_.op("scalar",
                                      lambda s, par=par, h=h: s.activation(
                                          out=kstag, in_=pgu[par],
                                          func=AF.Identity,
                                          bias=qkvob[:, 16 + h:17 + h]),
                                      reads=[preg, "qkvob"],
                                      writes=["kstag"])
                                S_.op("sync", lambda s, h=h, n=n: s.dma_start(
                                    out=kag_in[h * 128:(h + 1) * 128,
                                               n * 512:(n + 1) * 512],
                                    in_=kstag),
                                    reads=["kstag"],
                                    writes=[f"kagin{h}_{n}"], dma=True)
                                if h == H - 1 and n == 3:
                                    kin_regs = [f"kagin{hx}_{nx}"
                                                for hx in range(H)
                                                for nx in range(4)]
                                    S_.op("gpsimd",
                                          lambda g: g.collective_compute(
                                              "AllGather", ALU.bypass,
                                              replica_groups=[[0, 1, 2, 3],
                                                              [4, 5, 6, 7]],
                                              ins=[kag_in], outs=[kag]),
                                          reads=kin_regs, writes=["kag"],
                                          sem_override=cck)
                            else:
                                S_.op("scalar",
                                      lambda s, par=par, h=h: s.activation(
                                          out=vstag, in_=pgu[par],
                                          func=AF.Identity,
                                          bias=qkvob[:, 32 + h:33 + h]),
                                      reads=[preg, "qkvob"],
                                      writes=["vstag"])
                                for t4 in range(4):
                                    S_.op("tensor",
                                          lambda t, t4=t4:
                                          t.transpose(
                                              ptr[:, 0:128],
                                              vstag[:, t4 * 128:
                                                    (t4 + 1) * 128],
                                              ident[:, :]),
                                          reads=["vstag", "ident"],
                                          writes=["pex2"])
                                    S_.op("vector",
                                          lambda v, t4=t4:
                                          v.tensor_copy(
                                              out=vstg2[:, t4 * 128:
                                                        (t4 + 1) * 128],
                                              in_=ptr[:, 0:128]),
                                          reads=["pex2"],
                                          writes=[f"vstg2_{t4}"])
                                    tg = n * 4 + t4
                                    gid = h // 4
                                    hin = h % 4
                                    S_.op("sync",
                                          lambda s, gid=gid, hin=hin, tg=tg,
                                          t4=t4:
                                          s.dma_start(
                                              out=vag_in4[gid][
                                                  :, hin * 2048 + tg * 128:
                                                  hin * 2048 + (tg + 1) * 128],
                                              in_=vstg2[:, t4 * 128:
                                                        (t4 + 1) * 128]),
                                          reads=[f"vstg2_{t4}"],
                                          writes=[f"vagin{tg}_{h}"],
                                          dma=True)
                                    if hin == 3 and tg == 15:
                                        vin = [f"vagin{t_}_{h_}"
                                               for t_ in range(16)
                                               for h_ in range(gid * 4,
                                                               gid * 4 + 4)]
                                        S_.op("gpsimd",
                                              lambda g, gid=gid:
                                              g.collective_compute(
                                                  "AllGather", ALU.bypass,
                                                  replica_groups=[[0, 1, 2, 3],
                                                                  [4, 5, 6, 7]],
                                                  ins=[vag_in4[gid]],
                                                  outs=[vag4[gid]]),
                                              reads=vin,
                                              writes=[f"vag4_{gid}"],
                                              sem_override=(ccv if gid < 2
                                                            else ccv2))

            kbuf = [xnb[:, 0:8192], xnb[:, 8192:16384]]
            vbuf = [acc[:, :].bitcast(BF16)[:, 0:8192],
                    acc[:, :].bitcast(BF16)[:, 8192:16384]]
            attout = acc[:, :].bitcast(BF16)[:, 16384:20480]
            qtl = [gated[:, 2048 + i * 256: 2048 + (i + 1) * 256]
                   for i in range(2)]
            expB = [gated[:, 2560 + i * 1024: 2560 + (i + 1) * 1024]
                    for i in range(2)]
            tmpE = gated[:, 4608:5120]
            oevs = [gated[:, :].bitcast(F32)[:, 0:256],
                    gated[:, :].bitcast(F32)[:, 256:512]]
            maskt = silu_g[:, 0:2048]
            S_.op("gpsimd", lambda g: g.dma_start(
                out=maskt, in_=maskq), writes=["maskt"], dma=True)

            ss_ctr = [0]
            for qb in range(8):
                nk = qb + 1
                nu = 8 * nk
                for h in range(H):
                    hpar = h % 2
                    kb = kbuf[hpar]
                    vb = vbuf[hpar]
                    kreg = f"kbuf{hpar}"
                    vreg = f"vbuf{hpar}"
                    ktoks = []
                    vtoks = []
                    for qr in range(4):
                        ktoks.append(S_.op(
                            "sync", lambda s, qr=qr, h=h, nk=nk, kb=kb:
                            s.dma_start(
                                out=kb[:, qr * nk * 256:(qr + 1) * nk * 256],
                                in_=kag[qr * D + h * 128: qr * D + (h + 1) * 128,
                                        0:nk * 256]),
                            reads=["kag"],
                            writes=[kreg] if qr == 0 else (), dma=True))
                        gid = h // 4
                        hin = h % 4
                        vsrc = vag4[gid][qr * 128:(qr + 1) * 128,
                                         hin * 2048: hin * 2048 + nk * 256]
                        vdst = vb[:, qr * nk * 256:(qr + 1) * nk * 256]
                        vtoks.append(S_.op(
                            "sync", lambda s, vsrc=vsrc, vdst=vdst:
                            s.dma_start(out=vdst, in_=vsrc),
                            reads=[f"vag4_{gid}"],
                            writes=[vreg] if qr == 0 else (), dma=True))
                    S_.set_write(kreg, ktoks)
                    S_.set_write(vreg, vtoks)
                    qpar = h % 2
                    S_.op("sync", lambda s, h=h, qb=qb, qpar=qpar: s.dma_start(
                        out=qtl[qpar], in_=qscr[h * 128:(h + 1) * 128,
                                                qb * 256:(qb + 1) * 256]),
                        reads=[f"qscr{h}_{qb // 2}"],
                        writes=[f"qtl{qpar}"], dma=True)

                    l_last = o_last = None
                    u_seen = 0
                    for qr in range(4):
                        blk = 2 * nk
                        sizes = [4] * (blk // 4) + ([2] if blk % 4 else [])
                        v0 = 0
                        for bi, bs in enumerate(sizes):
                            sspar = ss_ctr[0] % 2
                            ss_ctr[0] += 1
                            pair = sspairs[sspar]
                            regs = pstRegs[sspar]
                            eb = expB[sspar]
                            ereg = f"expB{sspar}"
                            toks = [None, None]
                            for i in range(bs):
                                u = qr * blk + v0 + i
                                hi = i // 2
                                toks[hi] = S_.op(
                                    "tensor",
                                    lambda t, i=i, u=u, hi=hi, pair=pair,
                                    kb=kb, qpar=qpar:
                                    t.matmul(
                                        pair[hi][:, (i % 2) * 256:
                                                 (i % 2 + 1) * 256],
                                        kb[:, u * 128:(u + 1) * 128],
                                        qtl[qpar], start=True, stop=True),
                                    reads=[kreg, f"qtl{qpar}"],
                                    writes=[regs[hi]] if i % 2 == 0 else [])
                            for hi in range((bs + 1) // 2):
                                S_.set_write(regs[hi], [toks[hi]])
                            final = (bi == len(sizes) - 1)
                            nh = bs // 2
                            for hi in range(nh):
                                last_half = (hi == nh - 1)
                                if final and last_half:
                                    S_.op("scalar",
                                          lambda s, pair=pair, hi=hi:
                                          s.activation(
                                              out=tmpE[:, :],
                                              in_=pair[hi][:, 0:512],
                                              func=AF.Exp),
                                          reads=[regs[hi]], writes=["tmpE"])
                                    S_.op("vector",
                                          lambda v, eb=eb, hi=hi, qr=qr:
                                          v.tensor_tensor(
                                              out=eb[:, hi * 512:
                                                     (hi + 1) * 512],
                                              in0=tmpE[:, :],
                                              in1=maskt[:, qr * 512:
                                                        (qr + 1) * 512],
                                              op=ALU.mult),
                                          reads=["tmpE", "maskt"],
                                          writes=[ereg])
                                else:
                                    S_.op("scalar",
                                          lambda s, pair=pair, eb=eb, hi=hi:
                                          s.activation(
                                              out=eb[:, hi * 512:
                                                     (hi + 1) * 512],
                                              in_=pair[hi][:, 0:512],
                                              func=AF.Exp),
                                          reads=[regs[hi]], writes=[ereg])
                            for i in range(bs):
                                u = qr * blk + v0 + i
                                st = (u_seen + i == 0)
                                sp = (u_seen + i == nu - 1)
                                l_last = S_.op(
                                    "tensor",
                                    lambda t, i=i, eb=eb, st=st, sp=sp:
                                    t.matmul(
                                        psl, onescb[:, :],
                                        eb[:, i * 256:(i + 1) * 256],
                                        start=st, stop=sp),
                                    reads=["onescb", ereg],
                                    writes=["pdn1"] if st else [])
                            for i in range(bs):
                                u = qr * blk + v0 + i
                                st = (u_seen + i == 0)
                                sp = (u_seen + i == nu - 1)
                                o_last = S_.op(
                                    "tensor",
                                    lambda t, i=i, u=u, eb=eb, vb=vb,
                                    st=st, sp=sp:
                                    t.matmul(
                                        pso[:, 0:256],
                                        vb[:, u * 128:(u + 1) * 128],
                                        eb[:, i * 256:(i + 1) * 256],
                                        start=st, stop=sp),
                                    reads=[vreg, ereg],
                                    writes=["pdn0"] if st else [])
                            u_seen += bs
                            v0 += bs
                    S_.set_write("pdn1", [l_last])
                    S_.set_write("pdn0", [o_last])
                    oev = oevs[hpar]
                    oreg = "qstag" if hpar == 0 else "kstag"
                    S_.op("scalar", lambda s, oev=oev: s.copy(
                        out=oev, in_=pso[:, 0:256]),
                        reads=["pdn0"], writes=[oreg])
                    S_.op("vector", lambda v: v.reciprocal(
                        out=ltm[0:1, 0:256], in_=psl),
                        reads=["pdn1"], writes=["lrowA"])
                    S_.op("tensor", lambda t: t.matmul(
                        plinv[:, 0:256], onesr[:, :], ltm[0:1, 0:256],
                        start=True, stop=True),
                        reads=["onesr", "lrowA"], writes=["pbc"])
                    S_.op("vector", lambda v, h=h, oev=oev, oreg=oreg:
                          v.tensor_tensor(
                              out=attout[:, h * 256:(h + 1) * 256],
                              in0=oev, in1=plinv[:, 0:256],
                              op=ALU.mult),
                          reads=[oreg, "pbc"],
                          writes=[f"attout{h}"])
                for qtr in range(4):
                    emit_stage_dma(("proj", ow, qtr))
                    sid = stage_counter[0] - 1
                    consume_counter[0] = stage_counter[0]
                    slot = wslot[sid % 2]
                    sreg = f"wslot{sid % 2}"
                    for do in range(4):
                        do_g = qtr * 4 + do
                        par = do % 2
                        dstp = pbc if par == 0 else pex2
                        dreg = "pbc" if par == 0 else "pex2"
                        tok = None
                        for dc in range(NDC):
                            tok = S_.op(
                                "tensor",
                                lambda t, do=do, dc=dc, dstp=dstp, slot=slot:
                                t.matmul(
                                    dstp[:, 0:256],
                                    slot[:, dc * 512 + do * 128:
                                         dc * 512 + (do + 1) * 128],
                                    attout[:, dc * 256:(dc + 1) * 256],
                                    start=(dc == 0), stop=(dc == NDC - 1)),
                                reads=[sreg, f"attout{dc}"],
                                writes=[dreg] if dc == 0 else [])
                        S_.set_write(dreg, [tok])
                        S_.op("scalar", lambda s, dstp=dstp, do_g=do_g:
                              s.activation(
                                  out=xsq[:, 0:256], in_=dstp[:, 0:256],
                                  func=AF.Identity,
                                  bias=qkvob[:, 48 + do_g:49 + do_g]),
                              reads=[dreg, "qkvob"], writes=["xsqA"])
                        bpar = do_g % 2
                        S_.op("sync", lambda s, do_g=do_g, qb=qb, bpar=bpar:
                              s.dma_start(
                                  out=xch[bpar][:, 0:256],
                                  in_=x_scr[do_g * 128:(do_g + 1) * 128,
                                            qb * 256:(qb + 1) * 256]),
                              reads=[f"xscr{do_g}_{qb // 4}"],
                              writes=[f"xch{bpar}"], dma=True)
                        S_.op("vector", lambda v, bpar=bpar: v.tensor_tensor(
                            out=xsq[:, 256:512], in0=xsq[:, 0:256],
                            in1=xch[bpar][:, 0:256], op=ALU.add),
                            reads=["xsqA", f"xch{bpar}"], writes=["xsqB"])
                        S_.op("sync", lambda s, do_g=do_g, qb=qb: s.dma_start(
                            out=x_scr[do_g * 128:(do_g + 1) * 128,
                                      qb * 256:(qb + 1) * 256],
                            in_=xsq[:, 256:512]),
                            reads=["xsqB"],
                            writes=[f"xscr{do_g}_{qb // 4}"], dma=True)
            if do_f3:
                pf3 = moe_prefetch_list() + moe_prefetch_list()
                emit_stage_dma(pf3.pop(0))
                emit_stage_dma(pf3.pop(0))
            S_.fence()
            if dbg:
                qregs = [f"qscr{h}_{n}" for h in range(H) for n in range(4)]
                S_.op("gpsimd", lambda g: g.dma_start(out=dbgq, in_=qscr),
                      reads=qregs, writes=["dbgq"], dma=True)
                S_.op("gpsimd", lambda g: g.dma_start(out=dbgk, in_=kag),
                      reads=["kag"], writes=["dbgk"], dma=True)
                S_.op("gpsimd", lambda g: g.dma_start(
                      out=dbgv[0:512, :], in_=vag4[0]),
                      reads=["vag4_0"], writes=["dbgv"], dma=True)
                S_.wait_tokens("gpsimd", [S_.regions[r][0][0] for r in ("dbgq", "dbgk", "dbgv")])
            emit_stats_stream(x_scr, xscr_regs[0], 0)
            emit_stats_stream(x_scr, xscr_regs[1], 1)

        y_regs = [[f"y{dc}_{p}" for dc in range(NDC)] for p in range(2)]
        if do_f3:
            pf = pf3 if do_attn else None
            if pf is None:
                pf = moe_prefetch_list() + moe_prefetch_list()
                emit_stage_dma(pf.pop(0))
                emit_stage_dma(pf.pop(0))
            emit_moe_pass(0, x_scr, xscr_regs[0], y_out, y_regs[0], False, pf)
            emit_moe_pass(1, x_scr, xscr_regs[1], y_out, y_regs[1], False, pf)
            assert not pf
        else:
            for p in range(2):
                for dc in range(NDC):
                    S_.op("sync", lambda s, dc=dc, p=p: s.dma_start(
                        out=y_out[dc * 128:(dc + 1) * 128,
                                  p * TP:(p + 1) * TP],
                        in_=x_scr[dc * 128:(dc + 1) * 128,
                                  p * TP:(p + 1) * TP]),
                        reads=[f"xscr{dc}_{p}"],
                        writes=[f"y{dc}_{p}"], dma=True)

        ytoks = []
        for p in range(2):
            for dc in range(NDC):
                reg = S_.regions.get(f"y{dc}_{p}")
                if reg:
                    ytoks.extend(reg[0])
        S_.wait_tokens("sync", ytoks)

        if verbose:
            print(f"sched ops={S_.n_ops()} waits={S_.n_waits()}")
        with nc.Block() as block:
            S_.emit(block)

    return nc


def host_shard(x_full):
    out = []
    xr = np.asarray(x_full).reshape(B, 8, 4, 256, D)
    for r in range(8):
        b, j = r // 4, r % 4
        xc = xr[b, :, j].reshape(TLOC, D)
        out.append(np.ascontiguousarray(xc.T))
    return out


def host_mask(j):
    m = np.zeros((128, 2048), np.float32)
    for qr in range(4):
        for m2 in range(2):
            kk = m2 * 128 + np.arange(128)[:, None]
            qq = np.arange(256)[None, :]
            valid = (256 * qr + kk) <= (256 * j + qq)
            m[:, (qr * 2 + m2) * 256:(qr * 2 + m2 + 1) * 256] = valid
    return m


def kernel(L0_attn_x, L0_attn_q_w, L0_attn_q_b, L0_attn_k_w, L0_attn_k_b,
           L0_attn_v_w, L0_attn_v_b, L0_attn_o_w, L0_attn_o_b,
           L0_ffn_gate, L0_ffn_gate_up_proj, L0_ffn_down_proj):
    nc = build_program()

    xs = host_shard(np.asarray(L0_attn_x, dtype=np.float32))
    biases = np.stack([
        np.asarray(L0_attn_q_b, np.float32).reshape(16, 128).T,
        np.asarray(L0_attn_k_b, np.float32).reshape(16, 128).T,
        np.asarray(L0_attn_v_b, np.float32).reshape(16, 128).T,
        np.asarray(L0_attn_o_b, np.float32).reshape(16, 128).T,
    ], axis=1).reshape(128, 64)
    common = {
        "qw": np.ascontiguousarray(np.asarray(L0_attn_q_w, np.float32).T),
        "kw": np.ascontiguousarray(np.asarray(L0_attn_k_w, np.float32).T),
        "vw": np.ascontiguousarray(np.asarray(L0_attn_v_w, np.float32).T),
        "ow": np.ascontiguousarray(np.asarray(L0_attn_o_w, np.float32).T),
        "biases": np.ascontiguousarray(biases),
        "gate": np.ascontiguousarray(np.asarray(L0_ffn_gate, np.float32)),
        "gup": np.ascontiguousarray(np.asarray(L0_ffn_gate_up_proj, np.float32)),
        "down": np.ascontiguousarray(np.asarray(L0_ffn_down_proj, np.float32)),
    }
    in_maps = []
    for r in range(8):
        m = dict(common)
        m["x"] = xs[r]
        m["maskq"] = host_mask(r % 4)
        in_maps.append(m)

    res = run_bass_kernel_spmd(nc, in_maps, core_ids=list(range(8)))

    out = np.zeros((B, 8, 4, 256, D), np.float32)
    for r in range(8):
        b, j = r // 4, r % 4
        yc = np.asarray(res.results[r]["y"])
        out[b, :, j] = yc.T.reshape(8, 256, D)
    return out.reshape(B, SEQ, D)
